# version 04 (final)
# baseline (speedup 1.0000x reference)
import numpy as np
import ml_dtypes
import contextlib
import concourse.bass as bass
import concourse.mybir as mybir
from concourse.bass_utils import run_bass_kernel_spmd

F32 = mybir.dt.float32
BF16 = mybir.dt.bfloat16
AF = mybir.ActivationFunctionType
ALU = mybir.AluOpType
AX = mybir.AxisListType
BF = ml_dtypes.bfloat16

S, D = 8192, 1024
H, DK, NFC = 16, 64, 8
NT, TQ = 16, 128
NHALO = 32
NQ = NT * TQ
NQA = NQ + NHALO
EPS = 1e-5
DFF, NEXP = 4096, 8
PIECES = [(0, 512), (512, 512), (1024, 512), (1536, 512), (2048, 32)]
PIECEW = [512, 512, 512, 512, 128]


class Sem:
    def __init__(self, h):
        self.h = h
        self.n = 0


class Sched:
    ENGINES = ("sync", "gpsimd", "tensor", "vector", "scalar")

    def __init__(self):
        self.ops = []
        self.enabled = True

    def op(self, eng, fn, waits=(), incs=()):
        if not self.enabled:
            return
        ic = []
        for (s, d) in incs:
            s.n += d
            ic.append((s, d))
        self.ops.append((eng, fn, [(s, v) for (s, v) in waits if v > 0], ic))

    def emit(self, block):
        for E in self.ENGINES:
            ops = [o for o in self.ops if o[0] == E]
            if not ops:
                continue

            def body(eng, ops=ops):
                for (_, fn, waits, incs) in ops:
                    for (s, v) in waits:
                        eng.wait_ge(s.h, v)
                    inst = fn(eng) if fn is not None else None
                    for (s, d) in incs:
                        if inst is None:
                            inst = eng.nop()
                    inst = inst
                    for (s, d) in incs:
                        inst = inst.then_inc(s.h, d)
            getattr(block, E)(body)
        self.ops = []


def _fm(a):
    T = np.asarray(a).shape[0]
    return np.ascontiguousarray(
        np.asarray(a).T.reshape(NFC, 128, T).transpose(1, 0, 2).reshape(128, NFC * T))


def _unfm(a, T):
    return np.ascontiguousarray(
        np.asarray(a).reshape(128, NFC, T).transpose(1, 0, 2).reshape(D, T).T)


def tile_starts(r):
    return [(r + 4 * j) * TQ for j in range(NT)]


def halo_positions(r):
    hp = []
    for p0 in tile_starts(r):
        hp.extend((p0 - 1, p0 + TQ))
    return hp


def own_token_positions(r):
    pos = []
    for p0 in tile_starts(r):
        pos.extend(range(p0, p0 + TQ))
    return pos


def rope_tables(positions):
    pos = np.asarray(positions, np.float64)
    inv = 1.0 / (10000.0 ** (np.arange(0, DK, 2) / DK))
    ang = np.outer(inv, pos)
    cos = np.repeat(np.cos(ang), 2, axis=0)
    sin = np.repeat(np.sin(ang), 2, axis=0)
    return (np.ascontiguousarray(np.concatenate([cos, cos], 0).astype(BF)),
            np.ascontiguousarray(np.concatenate([sin, sin], 0).astype(BF)))


def rot_matrix_T():
    P = np.zeros((128, 128), np.float32)
    for i in range(0, 128, 2):
        P[i, i + 1] = -1.0
        P[i + 1, i] = 1.0
    return np.ascontiguousarray(P.T)


def host_prep_A(inputs):
    x = np.asarray(inputs["L0_attn_x"], np.float32)
    convw_t = np.ascontiguousarray(
        np.asarray(inputs["L0_ffn_conv_w"], np.float32).transpose(2, 1, 0))
    wqkv_T = np.ascontiguousarray(np.asarray(inputs["L0_ffn_wqkv"], np.float32).T)
    convb_c = np.ascontiguousarray(
        np.asarray(inputs["L0_ffn_conv_b"], np.float32).reshape(NFC, 128).T)
    bqkv_c = np.ascontiguousarray(
        np.asarray(inputs["L0_ffn_bqkv"], np.float32).reshape(24, 128).T)
    PT = rot_matrix_T().astype(BF)
    cosK, sinK = rope_tables(range(S))
    xfm = [_fm(x[s]) for s in range(2)]
    kk = np.arange(128)[:, None]
    qq = np.arange(128)[None, :]

    maps = []
    for c in range(8):
        s, r = divmod(c, 4)
        hp = halo_positions(r)
        xo = np.zeros((NQA, D), np.float32)
        xo[:NQ] = x[s][own_token_positions(r)]
        for i, p in enumerate(hp):
            if 0 <= p < S:
                xo[NQ + i] = x[s][p]
        pm = np.zeros((128, 8 * 256), BF)
        for krel in range(2):
            for w in range(4):
                blk = np.zeros((128, 256), np.float32)
                for half in range(2):
                    blk[:, half * 128:(half + 1) * 128] = (
                        (512 * krel + 128 * w + kk) <= (128 * (r + 4 * half) + qq))
                pm[:, (krel * 4 + w) * 256:(krel * 4 + w + 1) * 256] = blk.astype(BF)
        hm = np.zeros((128, 64 * NHALO), BF)
        for kc in range(64):
            for i, p in enumerate(hp):
                pe = p if 0 <= p <= S else (0 if p < 0 else S)
                hm[:, kc * NHALO + i] = ((128 * kc + kk[:, 0]) <= pe).astype(BF)
        hz = np.array([[1.0 if 0 <= p < S else 0.0 for p in hp]], np.float32)
        posq = own_token_positions(r) + [min(max(p, 0), S - 1) for p in hp]
        cosQ, sinQ = rope_tables(posq)
        maps.append({
            "x_seq": xfm[s], "x_own": _fm(xo),
            "Wq": np.asarray(inputs["L0_attn_Wq"], np.float32),
            "Wk": np.asarray(inputs["L0_attn_Wk"], np.float32),
            "Wv": np.asarray(inputs["L0_attn_Wv"], np.float32),
            "Wo": np.asarray(inputs["L0_attn_Wo"], np.float32),
            "convw_t": convw_t, "convb_c": convb_c,
            "wqkv_T": wqkv_T, "bqkv_c": bqkv_c,
            "PT": PT, "ident_b": np.eye(128, dtype=BF),
            "ident_f": np.eye(128, dtype=np.float32),
            "cosK": cosK, "sinK": sinK, "cosQ": cosQ, "sinQ": sinQ,
            "pairmask": pm, "halomask": hm, "halozero": hz,
        })
    return maps


_A_CACHE = {}
DEBUG_A = False
DEBUG_A2 = False
PHASES_A = 99
PH3SUB = 99


def build_launch_A():
    if "nc" in _A_CACHE:
        return _A_CACHE["nc"]
    nc = bass.Bass("TRN2", target_bir_lowering=False)
    dt = nc.dram_tensor
    x_seq = dt("x_seq", [128, NFC * S], F32, kind="ExternalInput").ap()
    x_own = dt("x_own", [128, NFC * NQA], F32, kind="ExternalInput").ap()
    Wq = dt("Wq", [D, D], F32, kind="ExternalInput").ap()
    Wk = dt("Wk", [D, D], F32, kind="ExternalInput").ap()
    Wv = dt("Wv", [D, D], F32, kind="ExternalInput").ap()
    Wo = dt("Wo", [D, D], F32, kind="ExternalInput").ap()
    convw_t = dt("convw_t", [3, D, D], F32, kind="ExternalInput").ap()
    convb_c = dt("convb_c", [128, NFC], F32, kind="ExternalInput").ap()
    wqkv_T = dt("wqkv_T", [D, 3 * D], F32, kind="ExternalInput").ap()
    bqkv_c = dt("bqkv_c", [128, 24], F32, kind="ExternalInput").ap()
    iPT = dt("PT", [128, 128], BF16, kind="ExternalInput").ap()
    iidb = dt("ident_b", [128, 128], BF16, kind="ExternalInput").ap()
    iidf = dt("ident_f", [128, 128], F32, kind="ExternalInput").ap()
    icosK = dt("cosK", [128, S], BF16, kind="ExternalInput").ap()
    isinK = dt("sinK", [128, S], BF16, kind="ExternalInput").ap()
    icosQ = dt("cosQ", [128, NQA], BF16, kind="ExternalInput").ap()
    isinQ = dt("sinQ", [128, NQA], BF16, kind="ExternalInput").ap()
    ipm = dt("pairmask", [128, 8 * 256], BF16, kind="ExternalInput").ap()
    ihm = dt("halomask", [128, 64 * NHALO], BF16, kind="ExternalInput").ap()
    ihz = dt("halozero", [1, NHALO], F32, kind="ExternalInput").ap()
    x1_out = dt("x1_out", [128, NFC * NQA], F32, kind="ExternalOutput").ap()
    xc_out = dt("xc_out", [128, NFC * NQ], F32, kind="ExternalOutput").ap()
    q2_out = dt("q2_out", [128, NFC * NQ], BF16, kind="ExternalOutput").ap()
    k2_out = dt("k2_out", [128, NFC * NQ], BF16, kind="ExternalOutput").ap()
    v2_out = dt("v2_out", [128, NFC * NQ], BF16, kind="ExternalOutput").ap()
    K_scr = dt("K_scr", [128, NFC * S], BF16, kind="Internal").ap()
    V_scr = dt("V_scr", [H, 128, 65 * 64], BF16, kind="Internal").ap()
    if DEBUG_A2:
        dbgKB = dt("dbgKB", [128, 2 * S], BF16, kind="ExternalOutput").ap()
        dbgS2 = dt("dbgS2", [1, 64], F32, kind="ExternalOutput").ap()
        dbgKS2 = dt("dbgKS2", [64, S], BF16, kind="ExternalOutput").ap()
        dbgC2 = dt("dbgC2", [128, NFC * NQA], BF16, kind="ExternalOutput").ap()
    if DEBUG_A:
        dbgQ = dt("dbgQ", [128, NFC * NQA], BF16, kind="ExternalOutput").ap()
        dbgK = dt("dbgK", [128, NFC * S], BF16, kind="ExternalOutput").ap()
        dbgV = dt("dbgV", [H, 128, 65 * 64], BF16, kind="ExternalOutput").ap()
        dbgC = dt("dbgC", [128, NFC * NQA], BF16, kind="ExternalOutput").ap()
        dbgM = dt("dbgM", [1, 64], F32, kind="ExternalOutput").ap()
        dbgS = dt("dbgS", [128, 128], F32, kind="ExternalOutput").ap()
        dbgKS = dt("dbgKS", [64, S], BF16, kind="ExternalOutput").ap()
        dbgQa = dt("dbgQa", [65, NQA], BF16, kind="ExternalOutput").ap()
        dbgXob = dt("dbgXob", [128, NFC * NQA], BF16, kind="ExternalOutput").ap()
        dbgP = dt("dbgP", [128, 2048], F32, kind="ExternalOutput").ap()
        dbgQ2 = dt("dbgQ2", [128, NFC * NQA], BF16, kind="ExternalOutput").ap()
        dbgQ3 = dt("dbgQ3", [128, NFC * NQA], BF16, kind="ExternalOutput").ap()

    x_seq3 = x_seq.rearrange("p (f s) -> p f s", f=NFC)
    x_own3 = x_own.rearrange("p (f s) -> p f s", f=NFC)
    K_scr3 = K_scr.rearrange("p (f s) -> p f s", f=NFC)
    x1_out3 = x1_out.rearrange("p (f s) -> p f s", f=NFC)
    xc_out3 = xc_out.rearrange("p (f s) -> p f s", f=NFC)

    sch = Sched()
    with contextlib.ExitStack() as stack:
        al = stack.enter_context
        sIn = Sem(al(nc.semaphore("sIn")))
        sW = Sem(al(nc.semaphore("sW")))
        sMM = Sem(al(nc.semaphore("sMM")))
        sEv = Sem(al(nc.semaphore("sEv")))
        sSc = Sem(al(nc.semaphore("sSc")))
        sScr = Sem(al(nc.semaphore("sScr")))
        sLd = Sem(al(nc.semaphore("sLd")))
        sOut = Sem(al(nc.semaphore("sOut")))
        cPT = al(nc.sbuf_tensor("cPT", [128, 128], BF16))
        cIdb = al(nc.sbuf_tensor("cIdb", [128, 128], BF16))
        cIdf = al(nc.sbuf_tensor("cIdf", [128, 128], F32))
        cPm = al(nc.sbuf_tensor("cPm", [128, 8 * 256], BF16))
        cHm = al(nc.sbuf_tensor("cHm", [128, 64 * NHALO], BF16))
        cHz = al(nc.sbuf_tensor("cHz", [1, NHALO], F32))
        cCosQ = al(nc.sbuf_tensor("cCosQ", [128, NQA], BF16))
        cSinQ = al(nc.sbuf_tensor("cSinQ", [128, NQA], BF16))
        ones64 = al(nc.sbuf_tensor("ones64", [64, 1], BF16))
        ones1r = al(nc.sbuf_tensor("ones1r", [1, 128], F32))
        onesq = al(nc.sbuf_tensor("onesq", [1, NQA], BF16))
        cBq = al(nc.sbuf_tensor("cBq", [128, 24], F32))
        cCb = al(nc.sbuf_tensor("cCb", [128, NFC], F32))
        big1 = al(nc.sbuf_tensor("big1", [128, NFC * NQA], BF16))
        scA = al(nc.sbuf_tensor("scA", [128, 512], F32))
        scB = al(nc.sbuf_tensor("scB", [128, 1024], F32))
        sst = al(nc.sbuf_tensor("sst", [1, 64], F32))

        for (dst, src) in ((cPT, iPT), (cIdb, iidb), (cIdf, iidf),
                           (cPm, ipm), (cHm, ihm), (cHz, ihz),
                           (cCosQ, icosQ), (cSinQ, isinQ),
                           (cBq, bqkv_c), (cCb, convb_c)):
            sch.op("sync", lambda e, d=dst, s=src: e.dma_start(out=d[:, :], in_=s[:, :]),
                   incs=[(sIn, 16)])
        c0 = sIn.n
        sch.op("vector", lambda e: e.memset(ones64[:, :], 1.0), waits=[(sIn, c0)])
        sch.op("vector", lambda e: e.memset(ones1r[:, :], 1.0))
        sch.op("vector", lambda e: e.memset(onesq[:, :], 1.0), incs=[(sEv, 1)])

        with (
            nc.sbuf_tensor("wKb", [128, NFC * 1024], BF16) as wKb,
            nc.sbuf_tensor("wVb", [128, NFC * 1024], BF16) as wVb,
            nc.sbuf_tensor("wQb", [128, NFC * 1024], BF16) as wQb,
            nc.sbuf_tensor("xg", [128, 2 * NFC * 512], BF16) as xg,
            nc.sbuf_tensor("cosg", [128, 2 * 512], BF16) as cosg,
            nc.sbuf_tensor("sing", [128, 2 * 512], BF16) as sing,
            nc.sbuf_tensor("ev1", [128, 512], BF16) as ev1,
            nc.sbuf_tensor("ev2", [128, 512], BF16) as ev2,
            nc.sbuf_tensor("ev3", [128, 512], BF16) as ev3,
            nc.sbuf_tensor("kg", [128, 2 * NFC * 512], BF16) as kg,
            nc.sbuf_tensor("stg", [128, 4 * 260], BF16) as stg,
            nc.sbuf_tensor("xob", [128, NFC * NQA], BF16) as xob,
            nc.sbuf_tensor("dbg_ps", [128, 2048], F32) as dbg_ps,
            nc.psum_tensor("psA", [128, 512], F32) as psA,
            nc.psum_tensor("psB", [128, 512], F32) as psB,
            nc.psum_tensor("psR", [128, 512], F32) as psR,
            nc.psum_tensor("psTb", [128, 512], BF16) as psTb,
        ):
            XGW = NFC * 512


            for (dst, src) in ((wKb, Wk), (wVb, Wv), (wQb, Wq)):
                sch.op("gpsimd",
                       lambda e, d=dst, s=src: e.dma_start(
                           out=d[:, :].rearrange("p (k o) -> p k o", k=NFC),
                           in_=s.rearrange("(k p) o -> p k o", p=128)),
                       incs=[(sW, 16)])
            sch.op("gpsimd", lambda e: e.dma_start(
                out=xob[:, :].rearrange("p (f s) -> p f s", f=NFC), in_=x_own3[:, :, :]),
                incs=[(sW, 16)])
            wAll = sW.n
            for b4 in range(4):
                sch.op("vector", lambda e, b4=b4: e.memset(
                    stg[:, b4 * 260: (b4 + 1) * 260].rearrange(
                        "p (c x) -> p c x", x=65)[:, :, 64:65], 1.0),
                    incs=([(sEv, 1)] if b4 == 3 else []))

            xg_rel = [0, 0]
            kg_rel = [0, 0]
            cs_rel = [0, 0]
            stg_rel = [0, 0]
            psab_rel = [0, 0]
            psR_rel = [0]

            def rope_combine(psrc, cos_ap, sin_ap, out_ap, n, slot):
                sch.op("vector", lambda e, n=n, psrc=psrc: e.tensor_copy(
                    ev1[:, :n], psrc[:, :n]), waits=[(sMM, sMM.n)], incs=[(sEv, 1)])
                psab_rel[slot] = sEv.n
                sch.op("tensor", lambda e, n=n: e.matmul(
                    psR[:, :n], cPT[:, :], ev1[:, :n], start=True, stop=True),
                    waits=[(sEv, sEv.n), (sEv, psR_rel[0])], incs=[(sMM, 1)])
                sch.op("vector", lambda e, n=n: e.tensor_copy(ev2[:, :n], psR[:, :n]),
                       waits=[(sMM, sMM.n)], incs=[(sEv, 1)])
                psR_rel[0] = sEv.n
                sch.op("vector", lambda e, n=n, c=cos_ap: e.tensor_mul(ev3[:, :n], ev1[:, :n], c))
                sch.op("vector", lambda e, n=n, s_=sin_ap: e.tensor_mul(ev2[:, :n], ev2[:, :n], s_))
                sch.op("vector", lambda e, n=n, o=out_ap: e.tensor_add(o, ev3[:, :n], ev2[:, :n]),
                       incs=[(sEv, 1)])

            for g in range(16):
                gb = g % 2
                sch.op("gpsimd", lambda e, g=g, gb=gb: e.dma_start(
                    out=xg[:, gb * XGW:(gb + 1) * XGW].rearrange("p (f s) -> p f s", f=NFC),
                    in_=x_seq3[:, :, g * 512:(g + 1) * 512]),
                    waits=[(sMM, xg_rel[gb])], incs=[(sW, 16)])
                sch.op("sync", lambda e, g=g, gb=gb: e.dma_start(
                    out=cosg[:, gb * 512:(gb + 1) * 512],
                    in_=icosK[:, g * 512:(g + 1) * 512]),
                    waits=[(sEv, cs_rel[gb])], incs=[(sW, 16)])
                sch.op("sync", lambda e, g=g, gb=gb: e.dma_start(
                    out=sing[:, gb * 512:(gb + 1) * 512],
                    in_=isinK[:, g * 512:(g + 1) * 512]),
                    incs=[(sW, 16)])
                wG = sW.n
                for fco in range(NFC):
                    ps = psA if fco % 2 == 0 else psB
                    for kc in range(NFC):
                        w0 = []
                        if kc == 0:
                            w0 = [(sEv, psab_rel[fco % 2])]
                            if fco == 0:
                                w0 += [(sW, wG), (sScr, kg_rel[gb])]
                                if g == 0:
                                    w0 += [(sIn, c0)]
                        sch.op("tensor", lambda e, ps=ps, kc=kc, fco=fco, gb=gb: e.matmul(
                            ps[:, :], wKb[:, kc * 1024 + fco * 128: kc * 1024 + fco * 128 + 128],
                            xg[:, gb * XGW + kc * 512: gb * XGW + (kc + 1) * 512],
                            start=(kc == 0), stop=(kc == NFC - 1)),
                            waits=w0,
                            incs=([(sMM, 1)] if kc == NFC - 1 else []))
                    rope_combine(ps, cosg[:, gb * 512:(gb + 1) * 512],
                                 sing[:, gb * 512:(gb + 1) * 512],
                                 kg[:, gb * XGW + fco * 512: gb * XGW + (fco + 1) * 512], 512, fco % 2)
                cs_rel[gb] = sEv.n
                sch.op("sync", lambda e, g=g, gb=gb: e.dma_start(
                    out=K_scr3[:, :, g * 512:(g + 1) * 512],
                    in_=kg[:, gb * XGW:(gb + 1) * XGW].rearrange("p (f s) -> p f s", f=NFC)),
                    waits=[(sEv, sEv.n)], incs=[(sScr, 16)])
                kg_rel[gb] = sScr.n
                for fco in range(NFC):
                    ps = psA if fco % 2 == 0 else psB
                    sb = fco % 2
                    for kc in range(NFC):
                        sch.op("tensor", lambda e, ps=ps, kc=kc, fco=fco, gb=gb: e.matmul(
                            ps[:, :], wVb[:, kc * 1024 + fco * 128: kc * 1024 + fco * 128 + 128],
                            xg[:, gb * XGW + kc * 512: gb * XGW + (kc + 1) * 512],
                            start=(kc == 0), stop=(kc == NFC - 1)),
                            waits=([(sEv, psab_rel[fco % 2])] if kc == 0 else []),
                            incs=([(sMM, 1)] if kc == NFC - 1 else []))
                    if fco == NFC - 1:
                        xg_rel[gb] = sMM.n
                    sch.op("vector", lambda e, ps=ps: e.tensor_copy(ev1[:, :], ps[:, :]),
                           waits=[(sMM, sMM.n)], incs=[(sEv, 1)])
                    psab_rel[fco % 2] = sEv.n
                    vEv = sEv.n
                    for w in range(4):
                        sch.op("tensor", lambda e, w=w: e.transpose(
                            psTb[:, w * 128:(w + 1) * 128],
                            ev1[:, w * 128:(w + 1) * 128], cIdb[:, :]),
                            waits=([(sEv, vEv), (sScr, stg_rel[sb])] if w == 0 else []),
                            incs=([(sMM, 1)] if w == 3 else []))
                    mmT = sMM.n
                    for w in range(4):
                        sch.op("vector", lambda e, w=w, sb=sb: e.tensor_copy(
                            stg[:, (sb * 2) * 260 + w * 65: (sb * 2) * 260 + w * 65 + 64],
                            psTb[:, w * 128: w * 128 + 64]),
                            waits=([(sMM, mmT)] if w == 0 else []))
                        sch.op("vector", lambda e, w=w, sb=sb: e.tensor_copy(
                            stg[:, (sb * 2 + 1) * 260 + w * 65: (sb * 2 + 1) * 260 + w * 65 + 64],
                            psTb[:, w * 128 + 64: w * 128 + 128]),
                            incs=([(sEv, 1)] if w == 3 else []))
                    stEv = sEv.n
                    sch.op("sync", lambda e, g=g, fco=fco, sb=sb: e.dma_start(
                        out=V_scr[2 * fco, :, g * 260:(g + 1) * 260],
                        in_=stg[:, (sb * 2) * 260:(sb * 2 + 1) * 260]),
                        waits=[(sEv, stEv)], incs=[(sScr, 16)])
                    sch.op("sync", lambda e, g=g, fco=fco, sb=sb: e.dma_start(
                        out=V_scr[2 * fco + 1, :, g * 260:(g + 1) * 260],
                        in_=stg[:, (sb * 2 + 1) * 260:(sb * 2 + 2) * 260]),
                        incs=[(sScr, 16)])
                    stg_rel[sb] = sScr.n
            qpi = 0
            for fco in range(NFC):
                for (off, n) in PIECES:
                    qps = psA if qpi % 2 == 0 else psB
                    qsl = qpi % 2
                    qpi += 1
                    for kc in range(NFC):
                        w0 = []
                        if kc == 0:
                            w0 = [(sEv, psab_rel[qsl])]
                            if fco == 0 and off == 0:
                                w0.append((sW, wAll))
                        sch.op("tensor", lambda e, kc=kc, fco=fco, off=off, n=n, qps=qps: e.matmul(
                            qps[:, :n],
                            wQb[:, kc * 1024 + fco * 128: kc * 1024 + fco * 128 + 128],
                            xob[:, kc * NQA + off: kc * NQA + off + n],
                            start=(kc == 0), stop=(kc == NFC - 1)),
                            waits=w0,
                            incs=([(sMM, 1)] if kc == NFC - 1 else []))
                    if DEBUG_A and fco == 0 and off in (512, 1024):
                        sch.op("vector", lambda e, off=off: e.tensor_copy(
                            dbg_ps[:, (off // 512 - 1) * 512:(off // 512) * 512], psA[:, :512]),
                            waits=[(sMM, sMM.n)], incs=[(sEv, 1)])
                    rope_combine(qps, cCosQ[:, off:off + n], cSinQ[:, off:off + n],
                                 big1[:, fco * NQA + off: fco * NQA + off + n], n, qsl)
                    if DEBUG_A and fco == 0 and off in (512, 1024):
                        sch.op("vector", lambda e, off=off: e.tensor_copy(
                            dbg_ps[:, 1024 + (off // 512 - 1) * 512:1024 + (off // 512) * 512],
                            ev1[:, :512]), incs=[(sEv, 1)])
            qDone = sEv.n
            qMM = sMM.n
            scrDone = sScr.n
            if DEBUG_A:
                sch.op("gpsimd", lambda e: e.dma_start(out=dbgXob[:, :], in_=xob[:, :]),
                       waits=[(sW, wAll)], incs=[(sOut, 16)])
                sch.op("gpsimd", lambda e: e.dma_start(out=dbgQ[:, :], in_=big1[:, :]),
                       waits=[(sEv, qDone)], incs=[(sOut, 16)])
                sch.op("gpsimd", lambda e: e.dma_start(out=dbgP[:, :], in_=dbg_ps[:, :]),
                       incs=[(sOut, 16)])
                sch.op("vector", None, waits=[(sOut, sOut.n)])
                sch.op("gpsimd", lambda e: e.dma_start(out=dbgK[:, :], in_=K_scr[:, :]),
                       waits=[(sScr, scrDone)], incs=[(sOut, 16)])
                sch.op("gpsimd", lambda e: e.dma_start(out=dbgV[:, :, :], in_=V_scr[:, :, :]),
                       incs=[(sOut, 16)])

        big2 = al(nc.sbuf_tensor("big2", [128, NFC * NQA], BF16))
        if PHASES_A < 3:
            sch.op("sync", None, waits=[(sOut, sOut.n), (sScr, sScr.n)])
            with nc.Block("launchA") as blk:
                sch.emit(blk)
            _A_CACHE["nc"] = nc
            return nc
        with (
            nc.sbuf_tensor("Kbuf", [128, 2 * S], BF16) as Kbuf,
            nc.sbuf_tensor("Vbuf", [128, 2 * 4160], BF16) as Vbuf,
            nc.sbuf_tensor("Qa", [65, NQA], BF16) as Qa,
            nc.sbuf_tensor("at", [128, 2 * 1024], BF16) as at,
            nc.sbuf_tensor("ksq", [64, S], BF16) as ksq,
            nc.sbuf_tensor("qsq", [64, NQA], BF16) as qsq,
            nc.sbuf_tensor("scP", [128, 160], F32) as scP,
            nc.psum_tensor("psS", [128, 2 * 1024], F32) as psS,
            nc.psum_tensor("psC", [65, 1024], F32) as psC,
            nc.psum_tensor("psM", [128, 512], F32) as psM,
        ):
            psCH = psM[0:65, 0:32]
            psN = psM[:, 32:128]
            psTT = psM[:, 128:256]
            psBc = psM[0:64, 256:512]

            if PH3SUB < 1:
                sch.op("sync", lambda e: e.dma_start(out=dbgQ2[:, :], in_=big1[:, :]),
                       waits=[(sEv, qDone), (sScr, sScr.n)], incs=[(sOut, 16)])
                sch.op("sync", None, waits=[(sOut, sOut.n)])
                with nc.Block("launchA") as blk:
                    sch.emit(blk)
                _A_CACHE["nc"] = nc
                return nc
            if PH3SUB != 10:
                sch.op("vector", lambda e: e.memset(Kbuf[64:65, :], -1.0), incs=[(sEv, 1)])
            else:
                sch.op("vector", None, incs=[(sEv, 1)])
            kbuf_rel = [0, 0]
            psS_rel = {0: 0, 1: 0}
            at_rel = {0: 0, 1: 0}
            psC_rel = [0, 0]
            psCH_rel = [0]
            qa_rel = 0

            for h in range(H):
                fc, half = h // 2, h % 2
                base = 64 * half
                hb = h % 2
                if PH3SUB != 11:
                    sch.op("gpsimd", lambda e, fc=fc, base=base, hb=hb: e.dma_start(
                        out=Kbuf[0:64, hb * S: hb * S + S], in_=K_scr[base:base + 64, fc * S:(fc + 1) * S]),
                        waits=[(sScr, scrDone), (sMM, max(kbuf_rel[hb], qMM)),
                               (sEv, qDone if h < 2 else 0)], incs=[(sLd, 16)])
                else:
                    sch.op("sync", None, incs=[(sLd, 16)])
                if PH3SUB != 12:
                    sch.op("sync", lambda e, h=h, hb=hb: e.dma_start(
                        out=Vbuf[:, hb * 4160:(hb + 1) * 4160], in_=V_scr[h]),
                        waits=([(sMM, qMM), (sEv, qDone)] if h < 2 else
                               [(sMM, kbuf_rel[hb])]), incs=[(sLd, 16)])
                else:
                    sch.op("sync", None, incs=[(sLd, 16)])
                ldH = sLd.n
                if DEBUG_A2 and h == 1:
                    sch.op("gpsimd", lambda e: e.dma_start(out=dbgKB[:, :], in_=Kbuf[:, :]),
                           waits=[(sLd, ldH)], incs=[(sOut, 16)])
                    for En in ("vector", "tensor", "scalar", "sync"):
                        sch.op(En, None, waits=[(sOut, sOut.n)])
                sch.op("vector", lambda e, hb=hb: e.memset(
                    Kbuf[64:65, hb * S: hb * S + S], -1.0),
                    waits=[(sLd, ldH)], incs=[(sEv, 1)])
                if PH3SUB != 13:
                    sch.op("vector", lambda e, fc=fc, base=base: e.tensor_copy(
                        Qa[0:64, :], big1[base:base + 64, fc * NQA:(fc + 1) * NQA]),
                        waits=[(sEv, qDone), (sMM, qa_rel)], incs=[(sEv, 1)])
                else:
                    sch.op("vector", None, waits=[(sEv, qDone), (sMM, qa_rel)], incs=[(sEv, 1)])
                if PH3SUB < 2 or PH3SUB in (10, 11, 12, 13):
                    sch.op("sync", lambda e: e.dma_start(out=dbgQ2[:, :], in_=big1[:, :]),
                           waits=[(sEv, sEv.n), (sLd, ldH)], incs=[(sOut, 16)])
                    sch.op("sync", None, waits=[(sOut, sOut.n), (sScr, sScr.n)])
                    with nc.Block("launchA") as blk:
                        sch.emit(blk)
                    _A_CACHE["nc"] = nc
                    return nc
                sch.op("vector", lambda e, hb=hb: e.tensor_mul(
                    ksq[:, :], Kbuf[0:64, hb * S: hb * S + S], Kbuf[0:64, hb * S: hb * S + S]),
                    waits=[(sLd, ldH)])
                sch.op("vector", lambda e: e.tensor_mul(
                    qsq[:, :], Qa[0:64, :], Qa[0:64, :]))
                sch.op("vector", lambda e: e.memset(psN[:, 80:81], 0.0))
                sch.op("vector", lambda e: e.memset(scP[:, 2:128], 0.0), incs=[(sEv, 1)])
                nEv = sEv.n
                for kc in range(64):
                    sch.op("tensor", lambda e, kc=kc: e.matmul(
                        psN[:, kc:kc + 1], ksq[:, kc * 128:(kc + 1) * 128],
                        ones64[:, :], start=True, stop=True),
                        waits=([(sEv, nEv)] if kc == 0 else []),
                        incs=([(sMM, 1)] if kc == 63 else []))
                for t in range(17):
                    n = 128 if t < 16 else 32
                    sch.op("tensor", lambda e, t=t, n=n: e.matmul(
                        psN[0:n, 64 + t:64 + t + 1],
                        qsq[:, t * 128: t * 128 + n], ones64[:, :],
                        start=True, stop=True),
                        incs=([(sMM, 1)] if t == 16 else []))
                mmN = sMM.n
                sch.op("vector", lambda e: e.reduce_max(scP[:, 0:1], psN[:, 0:64], axis=AX.X),
                       waits=[(sMM, mmN)])
                sch.op("vector", lambda e: e.reduce_max(scP[:, 1:2], psN[:, 64:81], axis=AX.X),
                       incs=[(sEv, 1)])
                sch.op("tensor", lambda e: e.transpose(psTT, scP[:, 0:128], cIdf[:, :]),
                       waits=[(sEv, sEv.n)], incs=[(sMM, 1)])
                sch.op("vector", lambda e: e.reduce_max(sst[:, 0:1], psM[0:1, 128:256], axis=AX.X),
                       waits=[(sMM, sMM.n)], incs=[(sEv, 1)])
                sch.op("tensor", lambda e: e.transpose(psTT, scP[:, 1:129], cIdf[:, :]),
                       waits=[(sEv, sEv.n)], incs=[(sMM, 1)])
                sch.op("vector", lambda e: e.reduce_max(sst[:, 1:2], psM[0:1, 128:256], axis=AX.X),
                       waits=[(sMM, sMM.n)], incs=[(sEv, 1)])
                if DEBUG_A2 and h == 1:
                    sch.op("gpsimd", lambda e: e.dma_start(out=dbgS2[:, :], in_=sst[:, :]),
                           waits=[(sEv, sEv.n)], incs=[(sOut, 16)])
                    sch.op("gpsimd", lambda e: e.dma_start(out=dbgKS2[:, :], in_=ksq[:, :]),
                           incs=[(sOut, 16)])
                    sch.op("gpsimd", lambda e: e.dma_start(
                        out=dbgKB[:, 0:2080], in_=big1[:, 0:2080]), incs=[(sOut, 16)])
                    sch.op("gpsimd", lambda e: e.dma_start(
                        out=dbgKB[0:65, 4096:6176], in_=Qa[:, :]), incs=[(sOut, 16)])
                    sch.op("gpsimd", lambda e: e.dma_start(
                        out=dbgKB[0:64, 8192:10272], in_=qsq[:, :]), incs=[(sOut, 16)])
                    for En in ("vector", "tensor", "scalar", "sync"):
                        sch.op(En, None, waits=[(sOut, sOut.n)])
                if DEBUG_A and h == 0:
                    sch.op("sync", lambda e: e.dma_start(out=dbgS[0:1, 0:64], in_=sst[:, :]),
                           waits=[(sEv, sEv.n)], incs=[(sOut, 16)])
                    sch.op("vector", None, waits=[(sOut, sOut.n)])
                sch.op("scalar", lambda e: e.activation(sst[:, 2:3], sst[:, 0:1], AF.Ln),
                       waits=[(sEv, sEv.n)], incs=[(sSc, 1)])
                sch.op("scalar", lambda e: e.activation(sst[:, 3:4], sst[:, 1:2], AF.Ln),
                       waits=[(sSc, sSc.n)], incs=[(sSc, 1)])
                sch.op("vector", lambda e: e.tensor_add(sst[:, 4:5], sst[:, 2:3], sst[:, 3:4]),
                       waits=[(sSc, sSc.n)], incs=[(sEv, 1)])
                sch.op("scalar", lambda e: e.activation(sst[:, 5:6], sst[:, 4:5], AF.Exp, scale=0.5),
                       waits=[(sEv, sEv.n)], incs=[(sSc, 1)])
                sch.op("vector", lambda e: e.tensor_scalar(
                    sst[:, 6:7], sst[:, 5:6], 0.125, None, ALU.mult),
                    waits=[(sSc, sSc.n)], incs=[(sEv, 1)])
                sch.op("vector", lambda e: e.tensor_scalar(
                    Qa[64:65, :], onesq[:, :], sst[:, 6:7], None, ALU.mult),
                    waits=[(sEv, sEv.n)], incs=[(sEv, 1)])
                sch.op("vector", lambda e: e.tensor_scalar(
                    Qa[0:64, :], Qa[0:64, :], 0.125, None, ALU.mult), incs=[(sEv, 1)])
                qaEv = sEv.n
                if DEBUG_A and h == 0:
                    sch.op("sync", lambda e: e.dma_start(out=dbgQ2[:, :], in_=big1[:, :]),
                           waits=[(sEv, sEv.n), (sMM, sMM.n), (sSc, sSc.n)], incs=[(sOut, 16)])
                    for En in ("vector", "tensor", "scalar", "gpsimd"):
                        sch.op(En, None, waits=[(sOut, sOut.n)])
                if DEBUG_A and h == 1:
                    sch.op("sync", lambda e: e.dma_start(out=dbgQ3[:, :], in_=big1[:, :]),
                           waits=[(sEv, sEv.n), (sMM, sMM.n), (sSc, sSc.n)], incs=[(sOut, 16)])
                    for En in ("vector", "tensor", "scalar", "gpsimd"):
                        sch.op(En, None, waits=[(sOut, sOut.n)])

                def window(qoff, nq, kb, sl, masks, hb):
                    for w in range(4):
                        sch.op("tensor", lambda e, kb=kb, w=w, qoff=qoff, nq=nq, sl=sl, hb=hb:
                               e.matmul(psS[:, sl * 1024 + w * nq: sl * 1024 + (w + 1) * nq],
                                        Kbuf[0:65, hb * S + kb * 512 + w * 128: hb * S + kb * 512 + w * 128 + 128],
                                        Qa[:, qoff:qoff + nq], start=True, stop=True),
                               waits=([(sEv, qaEv), (sSc, psS_rel[sl])] if w == 0 else []),
                               incs=([(sMM, 1)] if w == 3 else []))
                    sch.op("scalar", lambda e, sl=sl, nq=nq: e.activation(
                        at[:, sl * 1024: sl * 1024 + 4 * nq],
                        psS[:, sl * 1024: sl * 1024 + 4 * nq], AF.Exp),
                        waits=[(sMM, sMM.n), (sMM, at_rel[sl])], incs=[(sSc, 1)])
                    psS_rel[sl] = sSc.n
                    mEv = 0
                    for (w, m_ap) in masks:
                        sch.op("vector", lambda e, sl=sl, w=w, nq=nq, m_ap=m_ap: e.tensor_mul(
                            at[:, sl * 1024 + w * nq: sl * 1024 + (w + 1) * nq],
                            at[:, sl * 1024 + w * nq: sl * 1024 + (w + 1) * nq], m_ap),
                            waits=([(sSc, psS_rel[sl])] if w == masks[0][0] else []),
                            incs=([(sEv, 1)] if w == masks[-1][0] else []))
                        mEv = sEv.n
                    return mEv

                for m in range(8):
                    qoff = 256 * m
                    nkb = 2 * m + 2
                    cslot = m % 2
                    for kb in range(nkb):
                        sl = kb % 2
                        masks = []
                        if kb >= nkb - 2:
                            krel = kb - (nkb - 2)
                            masks = [(w, cPm[:, (krel * 4 + w) * 256:(krel * 4 + w + 1) * 256])
                                     for w in range(4)]
                        mEv = window(qoff, 256, kb, sl, masks, hb)
                        avw = [(sSc, psS_rel[sl])]
                        if masks:
                            avw.append((sEv, mEv))
                        if kb == 0:
                            avw.append((sEv, psC_rel[cslot]))
                        for w in range(4):
                            sch.op("tensor", lambda e, kb=kb, w=w, sl=sl, hb=hb, cslot=cslot,
                                   nkb=nkb: e.matmul(
                                       psC[:, cslot * 512: cslot * 512 + 256],
                                       Vbuf[:, hb * 4160 + (4 * kb + w) * 65: hb * 4160 + (4 * kb + w) * 65 + 65],
                                       at[:, sl * 1024 + w * 256: sl * 1024 + (w + 1) * 256],
                                       start=(kb == 0 and w == 0),
                                       stop=(kb == nkb - 1 and w == 3)),
                                   waits=(avw if w == 0 else []),
                                   incs=([(sMM, 1)] if kb == nkb - 1 and w == 3 else []))
                        at_rel[sl] = sMM.n
                    mmP = sMM.n
                    sch.op("vector", lambda e, cslot=cslot: e.reciprocal(
                        scA[0:1, 0:256], psC[64:65, cslot * 512: cslot * 512 + 256]),
                        waits=[(sMM, mmP)], incs=[(sEv, 1)])
                    sch.op("tensor", lambda e: e.matmul(
                        psBc[:, :], ones1r[0:1, 0:64], scA[0:1, 0:256],
                        start=True, stop=True), waits=[(sEv, sEv.n)], incs=[(sMM, 1)])
                    sch.op("vector", lambda e: e.tensor_copy(scB[0:64, 0:256], psBc[:, :]),
                           waits=[(sMM, sMM.n)])
                    sch.op("vector", lambda e, m=m, fc=fc, base=base, cslot=cslot: e.tensor_mul(
                        big2[base:base + 64, fc * NQA + 256 * m: fc * NQA + 256 * m + 256],
                        psC[0:64, cslot * 512: cslot * 512 + 256], scB[0:64, 0:256]),
                        incs=[(sEv, 1)])
                    psC_rel[cslot] = sEv.n
                for kb in range(16):
                    sl = kb % 2
                    masks = [(w, cHm[:, (4 * kb + w) * 32:(4 * kb + w) * 32 + 32])
                             for w in range(4)]
                    mEv = window(2048, 32, kb, sl, masks, hb)
                    avw = [(sSc, psS_rel[sl]), (sEv, mEv)]
                    if kb == 0:
                        avw.append((sEv, psCH_rel[0]))
                    for w in range(4):
                        sch.op("tensor", lambda e, kb=kb, w=w, sl=sl, hb=hb: e.matmul(
                            psCH,
                            Vbuf[:, hb * 4160 + (4 * kb + w) * 65: hb * 4160 + (4 * kb + w) * 65 + 65],
                            at[:, sl * 1024 + w * 32: sl * 1024 + (w + 1) * 32],
                            start=(kb == 0 and w == 0), stop=(kb == 15 and w == 3)),
                            waits=(avw if w == 0 else []),
                            incs=([(sMM, 1)] if kb == 15 and w == 3 else []))
                    at_rel[sl] = sMM.n
                mmH = sMM.n
                kbuf_rel[hb] = mmH
                qa_rel = mmH
                sch.op("vector", lambda e: e.reciprocal(scA[0:1, 0:32], psM[64:65, 0:32]),
                       waits=[(sMM, mmH)], incs=[(sEv, 1)])
                sch.op("tensor", lambda e: e.matmul(
                    psBc[:, 0:32], ones1r[0:1, 0:64], scA[0:1, 0:32],
                    start=True, stop=True), waits=[(sEv, sEv.n)], incs=[(sMM, 1)])
                sch.op("vector", lambda e: e.tensor_copy(scB[0:64, 0:32], psBc[:, 0:32]),
                       waits=[(sMM, sMM.n)])
                sch.op("vector", lambda e, fc=fc, base=base: e.tensor_mul(
                    big2[base:base + 64, fc * NQA + 2048: fc * NQA + 2080],
                    psM[0:64, 0:32], scB[0:64, 0:32]), incs=[(sEv, 1)])
                psCH_rel[0] = sEv.n
            attnEv = sEv.n
            attnMM = sMM.n
            if DEBUG_A2:
                sch.op("gpsimd", lambda e: e.dma_start(out=dbgC2[:, :], in_=big2[:, :]),
                       waits=[(sEv, attnEv)], incs=[(sOut, 16)])
                for En in ("vector", "tensor", "scalar", "sync"):
                    sch.op(En, None, waits=[(sOut, sOut.n)])
            if DEBUG_A:
                sch.op("gpsimd", lambda e: e.dma_start(out=dbgC[:, :], in_=big2[:, :]),
                       waits=[(sEv, attnEv)], incs=[(sOut, 16)])
                sch.op("gpsimd", lambda e: e.dma_start(out=dbgM[:, :], in_=sst[:, :]),
                       incs=[(sOut, 16)])

        with (
            nc.sbuf_tensor("wOb", [128, NFC * 1024], BF16) as wOb,
            nc.sbuf_tensor("xres", [128, NFC * 512], F32) as xres,
            nc.sbuf_tensor("vallp", [128, 2 * NFC * 512], F32) as vallp,
            nc.sbuf_tensor("vtm", [128, 1024], F32) as vtm,
            nc.sbuf_tensor("vsq", [128, 1024], F32) as vsq,
            nc.sbuf_tensor("x1tm", [128, 1024], F32) as x1tm,
            nc.sbuf_tensor("x1t", [128, NFC * 128], F32) as x1t,
            nc.psum_tensor("psA2", [128, 512], F32) as psA2,
            nc.psum_tensor("psB2", [128, 512], F32) as psB2,
            nc.psum_tensor("psT2", [128, 512], F32) as psT2,
            nc.psum_tensor("psBc2", [128, 256], F32) as psBc2,
        ):
            sch.op("gpsimd", lambda e: e.dma_start(
                out=wOb[:, :].rearrange("p (k o) -> p k o", k=NFC),
                in_=Wo.rearrange("(k p) o -> p k o", p=128)),
                waits=[(sMM, attnMM), (sEv, attnEv), (sSc, sSc.n)], incs=[(sW, 16)])
            wWo = sW.n
            xres_rel = 0
            vall_rel = [0, 0]
            ps2_rel = [0, 0]
            psT2_rel = [0]
            x1t_rel = [0]
            for pi, (off, n) in enumerate(PIECES):
                pb = pi % 2
                w = PIECEW[pi]
                vbase = pb * NFC * 512
                sch.op("sync", lambda e, off=off, n=n: e.dma_start(
                    out=xres[:, 0:NFC * n].rearrange("p (f s) -> p f s", f=NFC),
                    in_=x_own3[:, :, off:off + n]),
                    waits=[(sEv, max(xres_rel, attnEv)), (sMM, attnMM if off == 0 else 0)],
                    incs=[(sLd, 16)])
                ldX = sLd.n
                for fco in range(NFC):
                    ps = psA2 if fco % 2 == 0 else psB2
                    for kc in range(NFC):
                        w0 = []
                        if kc == 0:
                            w0 = [(sEv, ps2_rel[fco % 2])]
                            if fco == 0 and off == 0:
                                w0 += [(sW, wWo), (sEv, attnEv)]
                        sch.op("tensor", lambda e, ps=ps, kc=kc, fco=fco, off=off, n=n: e.matmul(
                            ps[:, :n],
                            wOb[:, kc * 1024 + fco * 128: kc * 1024 + fco * 128 + 128],
                            big2[:, kc * NQA + off: kc * NQA + off + n],
                            start=(kc == 0), stop=(kc == NFC - 1)),
                            waits=w0,
                            incs=([(sMM, 1)] if kc == NFC - 1 else []))
                    sch.op("vector", lambda e, ps=ps, fco=fco, n=n, w=w, vbase=vbase: e.tensor_add(
                        vallp[:, vbase + fco * w: vbase + fco * w + n],
                        ps[:, :n], xres[:, fco * n: fco * n + n]),
                        waits=[(sMM, sMM.n), (sLd, ldX), (sEv, vall_rel[pb])],
                        incs=[(sEv, 1)])
                    ps2_rel[fco % 2] = sEv.n
                xres_rel = sEv.n
                voDone = sEv.n
                ntiles = 4 if pi < 4 else 1
                for ti in range(ntiles):
                    t = pi * 4 + ti
                    n2 = 128 if t < 16 else 32
                    loff = ti * 128
                    for fc in range(NFC):
                        w0 = []
                        if fc % 4 == 0:
                            w0 = [(sEv, psT2_rel[0])]
                            if fc == 0 and ti == 0:
                                w0.append((sEv, voDone))
                        sch.op("tensor", lambda e, fc=fc, loff=loff, w=w, vbase=vbase: e.transpose(
                            psT2[:, (fc % 4) * 128:(fc % 4) * 128 + 128],
                            vallp[:, vbase + fc * w + loff: vbase + fc * w + loff + 128],
                            cIdf[:, :]),
                            waits=w0,
                            incs=([(sMM, 1)] if fc % 4 == 3 else []))
                        if fc % 4 == 3:
                            sch.op("vector", lambda e, fc=fc: e.tensor_copy(
                                vtm[:, (fc - 3) * 128:(fc + 1) * 128], psT2[:, :]),
                                waits=[(sMM, sMM.n)], incs=[(sEv, 1)])
                            psT2_rel[0] = sEv.n
                    tEv = sEv.n
                    if DEBUG_A2 and pi == 0 and ti == 0:
                        sch.op("gpsimd", lambda e: e.dma_start(
                            out=dbgKS2[0:64, 0:4096], in_=vallp[0:64, 0:4096]),
                            waits=[(sEv, tEv)], incs=[(sOut, 16)])
                        sch.op("gpsimd", lambda e: e.dma_start(
                            out=dbgKS2[0:64, 4096:5120], in_=vtm[0:64, 0:1024]),
                            incs=[(sOut, 16)])
                        for En in ("vector", "tensor", "scalar", "sync"):
                            sch.op(En, None, waits=[(sOut, sOut.n)])
                    sch.op("vector", lambda e, n2=n2: e.reduce_sum(
                        scB[0:n2, 0:1], vtm[0:n2, :], axis=AX.X),
                        waits=[(sEv, tEv)], incs=[(sEv, 1)])
                    sch.op("vector", lambda e, n2=n2: e.tensor_scalar(
                        scB[0:n2, 8:9], scB[0:n2, 0:1], -1.0 / D, None, ALU.mult),
                        waits=[(sEv, sEv.n)], incs=[(sEv, 1)])
                    sch.op("vector", lambda e, n2=n2: e.tensor_add(
                        x1tm[0:n2, :], vtm[0:n2, :],
                        scB[0:n2, 8:9].broadcast_to((n2, D))),
                        waits=[(sEv, sEv.n)], incs=[(sEv, 1)])
                    sch.op("vector", lambda e, n2=n2: e.tensor_mul(
                        vsq[0:n2, :], x1tm[0:n2, :], x1tm[0:n2, :]),
                        waits=[(sEv, sEv.n)], incs=[(sEv, 1)])
                    sch.op("vector", lambda e, n2=n2: e.reduce_sum(
                        scB[0:n2, 1:2], vsq[0:n2, :], axis=AX.X),
                        waits=[(sEv, sEv.n)], incs=[(sEv, 1)])
                    sch.op("vector", lambda e, n2=n2: e.tensor_scalar(
                        scB[0:n2, 4:5], scB[0:n2, 1:2], 1.0 / D, EPS, ALU.mult, ALU.add),
                        waits=[(sEv, sEv.n)], incs=[(sEv, 1)])
                    sch.op("scalar", lambda e, n2=n2: e.activation(
                        scB[0:n2, 7:8], scB[0:n2, 4:5], AF.Ln),
                        waits=[(sEv, sEv.n)], incs=[(sSc, 1)])
                    sch.op("scalar", lambda e, n2=n2: e.activation(
                        scB[0:n2, 3:4], scB[0:n2, 7:8], AF.Exp, scale=-0.5),
                        waits=[(sSc, sSc.n)], incs=[(sSc, 1)])
                    sch.op("vector", None, waits=[(sSc, sSc.n)], incs=[(sEv, 1)])
                    sch.op("vector", lambda e, n2=n2: e.tensor_mul(
                        x1tm[0:n2, :], x1tm[0:n2, :],
                        scB[0:n2, 3:4].broadcast_to((n2, D))),
                        waits=[(sEv, sEv.n)], incs=[(sEv, 1)])
                    apEv = sEv.n
                    if DEBUG_A2 and pi == 0 and ti == 0:
                        sch.op("gpsimd", lambda e: e.dma_start(
                            out=dbgKS2[0:64, 5120:6144], in_=x1tm[0:64, 0:1024]),
                            waits=[(sEv, apEv)], incs=[(sOut, 16)])
                        sch.op("gpsimd", lambda e: e.dma_start(
                            out=dbgKS2[0:64, 6144:6160], in_=scB[0:64, 0:16]),
                            incs=[(sOut, 16)])
                        for En in ("vector", "tensor", "scalar", "sync"):
                            sch.op(En, None, waits=[(sOut, sOut.n)])
                    for fc in range(NFC):
                        sch.op("tensor", lambda e, fc=fc: e.transpose(
                            psT2[:, (fc % 4) * 128:(fc % 4) * 128 + 128],
                            x1tm[:, fc * 128:(fc + 1) * 128], cIdf[:, :]),
                            waits=([(sEv, apEv), (sEv, psT2_rel[0]),
                                    (sOut, x1t_rel[0])] if fc % 4 == 0 else []),
                            incs=([(sMM, 1)] if fc % 4 == 3 else []))
                        if fc % 4 == 3:
                            for f2 in range(fc - 3, fc + 1):
                                sch.op("vector", lambda e, f2=f2, n2=n2: e.tensor_copy(
                                    x1t[:, f2 * 128: f2 * 128 + n2],
                                    psT2[:, (f2 % 4) * 128:(f2 % 4) * 128 + n2]),
                                    waits=([(sMM, sMM.n)] if f2 % 4 == 0 else []),
                                    incs=([(sEv, 1)] if f2 % 4 == 3 else []))
                            psT2_rel[0] = sEv.n
                    lnEv = sEv.n
                    if ti == ntiles - 1:
                        vall_rel[pb] = lnEv
                    if t < 16:
                        for fc in range(NFC):
                            sch.op("vector", lambda e, fc=fc, t=t: e.tensor_copy(
                                big1[:, fc * NQA + 130 * t + 1: fc * NQA + 130 * t + 129],
                                x1t[:, fc * 128:(fc + 1) * 128]),
                                incs=([(sEv, 1)] if fc == NFC - 1 else []))
                        sch.op("sync", lambda e, t=t: e.dma_start(
                            out=x1_out3[:, :, t * 128:(t + 1) * 128],
                            in_=x1t[:, :].rearrange("p (f s) -> p f s", f=NFC)),
                            waits=[(sEv, sEv.n)], incs=[(sOut, 16)])
                        x1t_rel[0] = sOut.n
                    else:
                        sch.op("tensor", lambda e: e.matmul(
                            psBc2[:, 0:32], ones1r[:, :], cHz[:, :], start=True, stop=True),
                            waits=[(sEv, lnEv)], incs=[(sMM, 1)])
                        for fc in range(NFC):
                            sch.op("vector", lambda e, fc=fc: e.tensor_mul(
                                x1t[:, fc * 128: fc * 128 + 32],
                                x1t[:, fc * 128: fc * 128 + 32], psBc2[:, 0:32]),
                                waits=([(sMM, sMM.n)] if fc == 0 else []),
                                incs=([(sEv, 1)] if fc == NFC - 1 else []))
                        for fc in range(NFC):
                            eL = big1[:, fc * NQA: fc * NQA + 2080].rearrange(
                                "p (t c) -> p t c", c=130)
                            hL = x1t[:, fc * 128: fc * 128 + 32].rearrange(
                                "p (i two) -> p i two", two=2)
                            sch.op("vector", lambda e, eL=eL, hL=hL: e.tensor_copy(
                                eL[:, :, 0:1], hL[:, :, 0:1]))
                            sch.op("vector", lambda e, eL=eL, hL=hL: e.tensor_copy(
                                eL[:, :, 129:130], hL[:, :, 1:2]),
                                incs=([(sEv, 1)] if fc == NFC - 1 else []))
                        sch.op("sync", lambda e: e.dma_start(
                            out=x1_out3[:, :, 2048:2080],
                            in_=x1t[:, :].rearrange("p (f s) -> p f s", f=NFC)[:, :, 0:32]),
                            waits=[(sEv, sEv.n)], incs=[(sOut, 16)])
            x1eEv = sEv.n

        with (
            nc.sbuf_tensor("cwb", [128, 24 * 1024], BF16) as cwb,
            nc.sbuf_tensor("xct", [128, 2 * NFC * 128], F32) as xct,
            nc.psum_tensor("psA3", [128, 512], F32) as psA3,
            nc.psum_tensor("psB3", [128, 512], F32) as psB3,
        ):
            sch.op("gpsimd", lambda e: e.dma_start(
                out=cwb[:, :].rearrange("p (t k o) -> p t k o", t=3, k=NFC),
                in_=convw_t.rearrange("t (k p) o -> p t k o", p=128)),
                waits=[(sMM, sMM.n), (sEv, x1eEv), (sSc, sSc.n)], incs=[(sW, 16)])
            wCv = sW.n
            xct_rel = [0, 0]
            ps3_rel = [0, 0]
            for t in range(16):
                tb = t % 2
                for fco in range(NFC):
                    ps = psA3 if fco % 2 == 0 else psB3
                    for ki in range(24):
                        tap, kc = divmod(ki, NFC)
                        w0 = []
                        if ki == 0:
                            w0 = [(sEv, ps3_rel[fco % 2])]
                            if fco == 0 and t == 0:
                                w0 += [(sW, wCv), (sEv, x1eEv)]
                        sch.op("tensor", lambda e, ps=ps, tap=tap, kc=kc, ki=ki,
                               t=t, fco=fco: e.matmul(
                                   ps[:, 0:128],
                                   cwb[:, ki * 1024 + fco * 128: ki * 1024 + fco * 128 + 128],
                                   big1[:, kc * NQA + 130 * t + tap: kc * NQA + 130 * t + tap + 128],
                                   start=(ki == 0), stop=(ki == 23)),
                               waits=w0,
                               incs=([(sMM, 1)] if ki == 23 else []))
                    sch.op("vector", lambda e, ps=ps, fco=fco, tb=tb: e.tensor_scalar(
                        xct[:, tb * 1024 + fco * 128: tb * 1024 + fco * 128 + 128],
                        ps[:, 0:128], cCb[:, fco:fco + 1], None, ALU.add),
                        waits=[(sMM, sMM.n)] + ([(sOut, xct_rel[tb])] if fco == 0 else []),
                        incs=[(sEv, 1)])
                    ps3_rel[fco % 2] = sEv.n
                    sch.op("vector", lambda e, ps=ps, fco=fco, t=t, tb=tb: e.tensor_copy(
                        big2[:, fco * NQA + t * 128: fco * NQA + t * 128 + 128],
                        xct[:, tb * 1024 + fco * 128: tb * 1024 + fco * 128 + 128]),
                        incs=[(sEv, 1)])
                sch.op("sync", lambda e, t=t, tb=tb: e.dma_start(
                    out=xc_out3[:, :, t * 128:(t + 1) * 128],
                    in_=xct[:, tb * 1024:(tb + 1) * 1024].rearrange("p (f s) -> p f s", f=NFC)),
                    waits=[(sEv, sEv.n)], incs=[(sOut, 16)])
                xct_rel[tb] = sOut.n
            xcEv = sEv.n

        with (
            nc.sbuf_tensor("wq3", [128, 2 * NFC * 128], BF16) as wq3,
            nc.sbuf_tensor("qkt", [128, 2 * 512], BF16) as qkt,
            nc.sbuf_tensor("bqs", [128, 8], F32) as bqs,
            nc.psum_tensor("psA4", [128, 512], F32) as psA4,
            nc.psum_tensor("psB4", [128, 512], F32) as psB4,
        ):
            sch.op("vector", lambda e: e.tensor_scalar(
                bqs[:, :], cBq[:, 0:8], 0.125, None, ALU.mult), incs=[(sEv, 1)])
            wq_rel = [0, 0]
            qkt_rel = [0, 0]
            ps4_rel = [0, 0]
            for oc in range(24):
                wb = oc % 2
                sch.op("gpsimd", lambda e, oc=oc, wb=wb: e.dma_start(
                    out=wq3[:, wb * 1024:(wb + 1) * 1024].rearrange("p (k o) -> p k o", k=NFC),
                    in_=wqkv_T.rearrange("(k p) o -> p k o", p=128)[
                        :, :, oc * 128:(oc + 1) * 128]),
                    waits=[(sMM, wq_rel[wb] if oc >= 2 else sMM.n), (sEv, sEv.n if oc < 2 else 0)],
                    incs=[(sW, 16)])
                wQ3 = sW.n
                for g in range(4):
                    ps = psA4 if g % 2 == 0 else psB4
                    qb = g % 2
                    for kc in range(NFC):
                        w0 = []
                        if kc == 0:
                            w0 = [(sEv, ps4_rel[g % 2])]
                            if g == 0:
                                w0.append((sW, wQ3))
                                if oc == 0:
                                    w0.append((sEv, xcEv))
                        sch.op("tensor", lambda e, ps=ps, kc=kc, oc=oc, g=g, wb=wb: e.matmul(
                            ps[:, :],
                            wq3[:, wb * 1024 + kc * 128: wb * 1024 + kc * 128 + 128],
                            big2[:, kc * NQA + g * 512: kc * NQA + g * 512 + 512],
                            start=(kc == 0), stop=(kc == NFC - 1)),
                            waits=w0,
                            incs=([(sMM, 1)] if kc == NFC - 1 else []))
                    if g == 3:
                        wq_rel[wb] = sMM.n
                    tgt = q2_out if oc < 8 else (k2_out if oc < 16 else v2_out)
                    if oc < 8:
                        sch.op("vector", lambda e, ps=ps, qb=qb, oc=oc: e.tensor_scalar(
                            qkt[:, qb * 512:(qb + 1) * 512], ps[:, :],
                            0.125, bqs[:, oc:oc + 1], ALU.mult, ALU.add),
                            waits=[(sMM, sMM.n), (sOut, qkt_rel[qb])], incs=[(sEv, 1)])
                    else:
                        sch.op("vector", lambda e, ps=ps, qb=qb, oc=oc: e.tensor_scalar(
                            qkt[:, qb * 512:(qb + 1) * 512], ps[:, :],
                            cBq[:, oc:oc + 1], None, ALU.add),
                            waits=[(sMM, sMM.n), (sOut, qkt_rel[qb])], incs=[(sEv, 1)])
                    ps4_rel[g % 2] = sEv.n
                    fcx = oc % 8
                    sch.op("sync", lambda e, tgt=tgt, fcx=fcx, g=g, qb=qb: e.dma_start(
                        out=tgt[:, fcx * NQ + g * 512: fcx * NQ + g * 512 + 512],
                        in_=qkt[:, qb * 512:(qb + 1) * 512]),
                        waits=[(sEv, sEv.n)], incs=[(sOut, 16)])
                    qkt_rel[qb] = sOut.n
            sch.op("sync", None, waits=[(sOut, sOut.n), (sScr, sScr.n)])

        with nc.Block("launchA") as blk:
            sch.emit(blk)
    _A_CACHE["nc"] = nc
    return nc


def run_A(inputs):
    maps = host_prep_A(inputs)
    nc = build_launch_A()
    res = run_bass_kernel_spmd(nc, maps, core_ids=list(range(8)))
    return res.results


def host_prep_B(inputs, resA):
    wo2T = np.ascontiguousarray(np.asarray(inputs["L0_ffn_wo"], np.float32).T)
    bo = np.ascontiguousarray(np.asarray(inputs["L0_ffn_bo"], np.float32).reshape(NFC, 128).T)
    gw = np.ascontiguousarray(np.asarray(inputs["L0_ffn_gate_w"], np.float32))
    gb = np.ascontiguousarray(np.asarray(inputs["L0_ffn_gate_b"], np.float32).reshape(NEXP, 1))
    w1 = np.ascontiguousarray(np.asarray(inputs["L0_ffn_w1"], np.float32))
    b1 = np.asarray(inputs["L0_ffn_b1"], np.float32)
    w2 = np.ascontiguousarray(np.asarray(inputs["L0_ffn_w2"], np.float32))
    b2f = np.ascontiguousarray(np.asarray(inputs["L0_ffn_b2"], np.float32))
    WwT = np.ascontiguousarray(np.asarray(inputs["L1_attn_W_write"], np.float32).T)
    b1c = np.ascontiguousarray(b1.reshape(NEXP, 32, 128).transpose(2, 0, 1).reshape(128, NEXP * 32))
    sel = np.zeros((NEXP, NEXP * 128), np.float32)
    for e in range(NEXP):
        sel[e, e * 128:(e + 1) * 128] = 1.0
    r11 = lambda k: np.ascontiguousarray(np.asarray(inputs[k], np.float32).reshape(1, D))
    u16 = lambda a: np.asarray(a).view(np.uint16)
    k2 = {}
    v2img = {}
    for s2 in range(2):
        kfull = np.zeros((128, NFC * S), np.uint16)
        vfull = np.zeros((S, D), np.uint16)
        for r in range(4):
            res = resA[s2 * 4 + r]
            own = np.asarray(own_token_positions(r))
            kk_ = u16(res["k2_out"])
            for fc in range(NFC):
                kfull[:, fc * S + own] = kk_[:, fc * NQ:(fc + 1) * NQ]
            vfull[own] = _unfm(u16(res["v2_out"]), NQ)
        k2[s2] = np.ascontiguousarray(kfull).view(BF)
        img = np.full((H, 128, 65 * 64), 0x3F80, np.uint16)
        vt = vfull.reshape(64, 128, H, 64)
        for hh in range(H):
            img3 = img[hh].reshape(128, 64, 65)
            img3[:, :, 0:64] = vt[:, :, hh, :].transpose(1, 0, 2)
        v2img[s2] = np.ascontiguousarray(img).view(BF)
    maps = []
    for c in range(8):
        s2, r = divmod(c, 4)
        res = resA[c]
        x1t_ = np.asarray(res["x1_out"]).reshape(128, NFC, NQA)[:, :, :NQ].reshape(128, NFC * NQ)
        maps.append({
            "q2": u16(res["q2_out"]).view(BF), "K2": k2[s2], "V2img": v2img[s2],
            "x1f": np.ascontiguousarray(x1t_), "xcf": np.asarray(res["xc_out"]),
            "ln1g_r": r11("L0_ffn_ln1_g"), "ln1b_r": r11("L0_ffn_ln1_b"),
            "ln2g_r": r11("L0_ffn_ln2_g"), "ln2b_r": r11("L0_ffn_ln2_b"),
            "wo2T": wo2T, "bo": bo, "gatew": gw, "gateb": gb, "sel": sel,
            "w1": w1, "b1c": b1c, "w2": w2, "b2f": b2f, "WwT": WwT,
            "identf": np.eye(128, dtype=np.float32),
        })
    return maps


_B_CACHE = {}
PHASES_B = 99
PH2SUB = 99


def build_launch_B():
    if "nc" in _B_CACHE:
        return _B_CACHE["nc"]
    nc = bass.Bass("TRN2", target_bir_lowering=False)
    dt = nc.dram_tensor
    q2 = dt("q2", [128, NFC * NQ], BF16, kind="ExternalInput").ap()
    K2 = dt("K2", [128, NFC * S], BF16, kind="ExternalInput").ap()
    V2img = dt("V2img", [H, 128, 65 * 64], BF16, kind="ExternalInput").ap()
    x1f = dt("x1f", [128, NFC * NQ], F32, kind="ExternalInput").ap()
    xcf = dt("xcf", [128, NFC * NQ], F32, kind="ExternalInput").ap()
    iG1 = dt("ln1g_r", [1, D], F32, kind="ExternalInput").ap()
    iB1 = dt("ln1b_r", [1, D], F32, kind="ExternalInput").ap()
    iG2 = dt("ln2g_r", [1, D], F32, kind="ExternalInput").ap()
    iB2r = dt("ln2b_r", [1, D], F32, kind="ExternalInput").ap()
    wo2T = dt("wo2T", [D, D], F32, kind="ExternalInput").ap()
    ibo = dt("bo", [128, NFC], F32, kind="ExternalInput").ap()
    igw = dt("gatew", [D, NEXP], F32, kind="ExternalInput").ap()
    igb = dt("gateb", [NEXP, 1], F32, kind="ExternalInput").ap()
    isel = dt("sel", [NEXP, NEXP * 128], F32, kind="ExternalInput").ap()
    w1i = dt("w1", [NEXP, D, DFF], F32, kind="ExternalInput").ap()
    ib1c = dt("b1c", [128, NEXP * 32], F32, kind="ExternalInput").ap()
    w2i = dt("w2", [NEXP, DFF, D], F32, kind="ExternalInput").ap()
    ib2f = dt("b2f", [NEXP, D], F32, kind="ExternalInput").ap()
    WwTi = dt("WwT", [D, D], F32, kind="ExternalInput").ap()
    iidf = dt("identf", [128, 128], F32, kind="ExternalInput").ap()
    x2_out = dt("x2_out", [128, NFC * NQ], F32, kind="ExternalOutput").ap()
    v_out = dt("v_out", [128, NFC * NQ], BF16, kind="ExternalOutput").ap()
    xi_scr = dt("xi_scr", [128, NFC * NQ], F32, kind="Internal").ap()
    xcf3 = xcf.rearrange("p (f s) -> p f s", f=NFC)
    x1f3 = x1f.rearrange("p (f s) -> p f s", f=NFC)
    xi_scr3 = xi_scr.rearrange("p (f s) -> p f s", f=NFC)

    sch = Sched()
    with contextlib.ExitStack() as stack:
        al = stack.enter_context
        sIn = Sem(al(nc.semaphore("sIn")))
        sW = Sem(al(nc.semaphore("sW")))
        sMM = Sem(al(nc.semaphore("sMM")))
        sEv = Sem(al(nc.semaphore("sEv")))
        sSc = Sem(al(nc.semaphore("sSc")))
        sLd = Sem(al(nc.semaphore("sLd")))
        sOut = Sem(al(nc.semaphore("sOut")))
        cIdf = al(nc.sbuf_tensor("cIdf", [128, 128], F32))
        ones64 = al(nc.sbuf_tensor("ones64", [64, 1], BF16))
        ones1r = al(nc.sbuf_tensor("ones1r", [1, 128], F32))
        cBo = al(nc.sbuf_tensor("cBo", [128, NFC], F32))
        cGb = al(nc.sbuf_tensor("cGb", [NEXP, 1], F32))
        cB1 = al(nc.sbuf_tensor("cB1", [128, NEXP * 32], F32))
        scB = al(nc.sbuf_tensor("scB", [128, 1024], F32))
        sst = al(nc.sbuf_tensor("sst", [1, 64], F32))
        scP = al(nc.sbuf_tensor("scP", [128, 160], F32))
        probs = al(nc.sbuf_tensor("probs", [NEXP, NQ], F32))

        for (dstn, srcn) in ((cIdf, iidf), (cBo, ibo), (cGb, igb),
                             (cB1, ib1c)):
            sch.op("sync", lambda e, d=dstn, s2=srcn: e.dma_start(out=d[:, :], in_=s2[:, :]),
                   incs=[(sIn, 16)])
        c0 = sIn.n
        sch.op("vector", lambda e: e.memset(ones64[:, :], 1.0), waits=[(sIn, c0)])
        sch.op("vector", lambda e: e.memset(ones1r[:, :], 1.0))
        sch.op("vector", lambda e: e.memset(ones1r[:, :], 1.0), incs=[(sEv, 1)])
        ev0 = sEv.n
        xi_bf = al(nc.sbuf_tensor("xi_bf", [128, NFC * NQ], BF16))
        acc = al(nc.sbuf_tensor("acc", [128, NFC * NQ], F32))

        with contextlib.ExitStack() as stBC:
            bigC = stBC.enter_context(nc.sbuf_tensor("bigC", [128, NFC * NQ], BF16))
            with contextlib.ExitStack() as stP1:
                Kbuf = stP1.enter_context(nc.sbuf_tensor("Kbuf", [128, S], BF16))
                Vbuf = stP1.enter_context(nc.sbuf_tensor("Vbuf", [128, 4160], BF16))
                Qa = stP1.enter_context(nc.sbuf_tensor("Qa", [65, NQ], BF16))
                at = stP1.enter_context(nc.sbuf_tensor("at", [128, 2 * 512], BF16))
                ksq = stP1.enter_context(nc.sbuf_tensor("ksq", [64, 2048], BF16))
                onesq = stP1.enter_context(nc.sbuf_tensor("onesq", [1, NQ], BF16))
                psS = stP1.enter_context(nc.psum_tensor("psS", [128, 2 * 512], F32))
                psC = stP1.enter_context(nc.psum_tensor("psC", [128, 1024], F32))
                psBc2 = stP1.enter_context(nc.psum_tensor("psBc2", [64, 512], F32))
                psM = stP1.enter_context(nc.psum_tensor("psM", [128, 512], F32))
                psN = psM[:, 32:128]
                sch.op("vector", lambda e: e.memset(onesq[:, :], 1.0))
                sch.op("vector", lambda e: e.memset(Kbuf[64:65, :], -1.0),
                       waits=[(sEv, ev0)], incs=[(sEv, 1)])
                ev1 = sEv.n
                kbuf_rel = [0, 0]
                kbuf_relE = [0, 0]
                psS_rel = {0: 0, 1: 0}
                at_rel = {0: 0, 1: 0}
                psC_rel = [0, 0]
                qa_rel = 0
                qa_relE = 0
                prev_mmN = [0]
                for h in range(H):
                    fc, half = h // 2, h % 2
                    base = 64 * half
                    hb = 0
                    sch.op("gpsimd", lambda e, fc=fc, base=base, hb=hb: e.dma_start(
                        out=Kbuf[0:64, hb * S: hb * S + S],
                        in_=K2[base:base + 64, fc * S:(fc + 1) * S]),
                        waits=[(sMM, kbuf_rel[hb]), (sEv, max(kbuf_relE[hb], ev1))],
                        incs=[(sLd, 16)])
                    sch.op("gpsimd", lambda e, fc=fc, base=base: e.dma_start(
                        out=Qa[0:64, :],
                        in_=q2[base:base + 64, fc * NQ:(fc + 1) * NQ]),
                        waits=[(sMM, qa_rel), (sEv, qa_relE)], incs=[(sLd, 16)])
                    sch.op("sync", lambda e, h=h, hb=hb: e.dma_start(
                        out=Vbuf[:, hb * 4160:(hb + 1) * 4160], in_=V2img[h]),
                        waits=[(sMM, kbuf_rel[hb])], incs=[(sLd, 16)])
                    ldH = sLd.n
                    for kp in range(4):
                        sch.op("vector", lambda e, hb=hb, kp=kp: e.tensor_mul(
                            ksq[:, :], Kbuf[0:64, hb * S + kp * 2048: hb * S + kp * 2048 + 2048],
                            Kbuf[0:64, hb * S + kp * 2048: hb * S + kp * 2048 + 2048]),
                            waits=[(sLd, ldH), (sMM, prev_mmN[0] if kp == 0 else sMM.n)],
                            incs=[(sEv, 1)])
                        kpEv = sEv.n
                        for kc in range(16):
                            sch.op("tensor", lambda e, kc=kc, kp=kp: e.matmul(
                                psN[:, kp * 16 + kc: kp * 16 + kc + 1],
                                ksq[:, kc * 128:(kc + 1) * 128],
                                ones64[:, :], start=True, stop=True),
                                waits=([(sEv, kpEv)] if kc == 0 else []),
                                incs=([(sMM, 1)] if kc == 15 else []))
                    sch.op("vector", lambda e: e.tensor_mul(
                        ksq[:, :], Qa[0:64, :], Qa[0:64, :]),
                        waits=[(sMM, sMM.n)], incs=[(sEv, 1)])
                    sch.op("vector", lambda e: e.memset(scP[:, 2:128], 0.0), incs=[(sEv, 1)])
                    nEv = sEv.n
                    for t in range(16):
                        sch.op("tensor", lambda e, t=t: e.matmul(
                            psN[:, 64 + t:64 + t + 1],
                            ksq[:, t * 128:(t + 1) * 128], ones64[:, :],
                            start=True, stop=True),
                            waits=([(sEv, nEv)] if t == 0 else []),
                            incs=([(sMM, 1)] if t == 15 else []))
                    mmN = sMM.n
                    prev_mmN[0] = mmN
                    sch.op("vector", lambda e: e.reduce_max(scP[:, 0:1], psN[:, 0:64], axis=AX.X),
                           waits=[(sMM, mmN)], incs=[(sEv, 1)])
                    sch.op("vector", lambda e: e.reduce_max(scP[:, 1:2], psN[:, 64:80], axis=AX.X),
                           waits=[(sEv, sEv.n)], incs=[(sEv, 1)])
                    sch.op("tensor", lambda e: e.transpose(psM[:, 128:256], scP[:, 0:128], cIdf[:, :]),
                           waits=[(sEv, sEv.n)], incs=[(sMM, 1)])
                    sch.op("vector", lambda e: e.reduce_max(sst[:, 0:1], psM[0:1, 128:256], axis=AX.X),
                           waits=[(sMM, sMM.n)], incs=[(sEv, 1)])
                    sch.op("tensor", lambda e: e.transpose(psM[:, 128:256], scP[:, 1:129], cIdf[:, :]),
                           waits=[(sEv, sEv.n)], incs=[(sMM, 1)])
                    sch.op("vector", lambda e: e.reduce_max(sst[:, 1:2], psM[0:1, 128:256], axis=AX.X),
                           waits=[(sMM, sMM.n)], incs=[(sEv, 1)])
                    sch.op("scalar", lambda e: e.activation(sst[:, 2:3], sst[:, 0:1], AF.Ln),
                           waits=[(sEv, sEv.n)], incs=[(sSc, 1)])
                    sch.op("scalar", lambda e: e.activation(sst[:, 3:4], sst[:, 1:2], AF.Ln),
                           waits=[(sSc, sSc.n)], incs=[(sSc, 1)])
                    sch.op("vector", lambda e: e.tensor_add(sst[:, 4:5], sst[:, 2:3], sst[:, 3:4]),
                           waits=[(sSc, sSc.n)], incs=[(sEv, 1)])
                    sch.op("scalar", lambda e: e.activation(sst[:, 5:6], sst[:, 4:5], AF.Exp, scale=0.5),
                           waits=[(sEv, sEv.n)], incs=[(sSc, 1)])
                    sch.op("vector", lambda e: e.tensor_scalar(
                        Qa[64:65, :], onesq[:, :], sst[:, 5:6], None, ALU.mult),
                        waits=[(sSc, sSc.n)], incs=[(sEv, 1)])
                    qaEv = sEv.n
                    qa_relE = qaEv
                    kbuf_relE[hb] = nEv
                    for qh in range(4):
                        qoff = qh * 512
                        cslot = qh % 2
                        pend = None
                        for kc64 in range(64):
                            sl = kc64 % 2
                            sch.op("tensor", lambda e, kc64=kc64, qoff=qoff, sl=sl, hb=hb: e.matmul(
                                psS[:, sl * 512:(sl + 1) * 512],
                                Kbuf[0:65, hb * S + kc64 * 128: hb * S + kc64 * 128 + 128],
                                Qa[:, qoff: qoff + 512],
                                start=True, stop=True),
                                waits=[(sEv, qaEv), (sSc, psS_rel[sl])], incs=[(sMM, 1)])
                            mmQK = sMM.n
                            if pend is not None:
                                pend()
                                pend = None
                            sch.op("scalar", lambda e, sl=sl: e.activation(
                                at[:, sl * 512:(sl + 1) * 512],
                                psS[:, sl * 512:(sl + 1) * 512], AF.Exp),
                                waits=[(sMM, mmQK), (sMM, at_rel[sl])], incs=[(sSc, 1)])
                            psS_rel[sl] = sSc.n

                            def mk_av(kc64=kc64, sl=sl, hb=hb, cslot=cslot, scw=sSc.n):
                                avw = [(sSc, scw)]
                                if kc64 == 0:
                                    avw.append((sEv, psC_rel[cslot]))
                                sch.op("tensor", lambda e, kc64=kc64, sl=sl, hb=hb, cslot=cslot: e.matmul(
                                    psC[0:65, cslot * 512:(cslot + 1) * 512],
                                    Vbuf[:, hb * 4160 + kc64 * 65: hb * 4160 + kc64 * 65 + 65],
                                    at[:, sl * 512:(sl + 1) * 512],
                                    start=(kc64 == 0), stop=(kc64 == 63)),
                                    waits=avw, incs=[(sMM, 1)])
                                at_rel[sl] = sMM.n
                            pend = mk_av
                        pend()
                        mmP = sMM.n
                        sch.op("vector", lambda e, cslot=cslot: e.reciprocal(
                            scB[0:1, 0:512], psC[64:65, cslot * 512:(cslot + 1) * 512]),
                            waits=[(sMM, mmP)], incs=[(sEv, 1)])
                        sch.op("tensor", lambda e: e.matmul(
                            psBc2[:, :], ones1r[0:1, 0:64], scB[0:1, 0:512],
                            start=True, stop=True), waits=[(sEv, sEv.n)], incs=[(sMM, 1)])
                        sch.op("vector", lambda e: e.tensor_copy(scB[0:64, 0:512], psBc2[:, :]),
                               waits=[(sMM, sMM.n)], incs=[(sEv, 1)])
                        sch.op("vector", lambda e, fc=fc, base=base, qoff=qoff, cslot=cslot:
                               e.tensor_mul(
                                   bigC[base:base + 64, fc * NQ + qoff: fc * NQ + qoff + 512],
                                   psC[0:64, cslot * 512:(cslot + 1) * 512], scB[0:64, 0:512]),
                               waits=[(sEv, sEv.n)], incs=[(sEv, 1)])
                        psC_rel[cslot] = sEv.n
                    kbuf_rel[hb] = sMM.n
                    qa_rel = sMM.n
                attnEv = sEv.n
                attnMM = sMM.n
                attnSc = sSc.n
            with contextlib.ExitStack() as stP3:
                sch.enabled = PHASES_B >= 2
                wOb = stP3.enter_context(nc.sbuf_tensor("wOb", [128, NFC * 1024], BF16))
                gwb = stP3.enter_context(nc.sbuf_tensor("gwb", [128, NFC * 128], BF16))
                xres = stP3.enter_context(nc.sbuf_tensor("xres", [128, NFC * 512], F32))
                vtm = stP3.enter_context(nc.sbuf_tensor("vtm", [128, 1024], F32))
                x1tm = stP3.enter_context(nc.sbuf_tensor("x1tm", [128, 1024], F32))
                xt8 = stP3.enter_context(nc.sbuf_tensor("xt8", [128, 1024], F32))
                vsq = xt8
                g1bc = stP3.enter_context(nc.sbuf_tensor("g1bc", [128, 1024], F32))
                b1bc = stP3.enter_context(nc.sbuf_tensor("b1bc", [128, 1024], F32))
                psA2 = stP3.enter_context(nc.psum_tensor("psA2", [128, 512], F32))
                psB2 = stP3.enter_context(nc.psum_tensor("psB2", [128, 512], F32))
                psT2 = stP3.enter_context(nc.psum_tensor("psT2", [128, 512], F32))
                psG = stP3.enter_context(nc.psum_tensor("psG", [NEXP, 512], F32))
                psGT = stP3.enter_context(nc.psum_tensor("psGT", [128, 128], F32))
                bnd = [(sMM, attnMM), (sEv, attnEv), (sSc, attnSc), (sLd, sLd.n)]
                sch.op("gpsimd", lambda e: e.dma_start(
                    out=wOb[:, :].rearrange("p (k o) -> p k o", k=NFC),
                    in_=wo2T.rearrange("(k p) o -> p k o", p=128)),
                    waits=bnd, incs=[(sW, 16)])
                sch.op("gpsimd", lambda e: e.dma_start(
                    out=gwb[:, :].rearrange("p (k o) -> p k o", k=NFC)[:, :, 0:NEXP],
                    in_=igw.rearrange("(k p) o -> p k o", p=128)), incs=[(sW, 16)])
                sch.op("gpsimd", lambda e: e.dma_start(out=scB[0:1, 0:1024], in_=iG1[:, :]),
                       incs=[(sW, 16)])
                wWo = sW.n
                for gi, dstb in ((0, g1bc), (1, b1bc)):
                    if gi == 1:
                        sch.op("gpsimd", lambda e: e.dma_start(
                            out=scB[0:1, 0:1024], in_=iB1[:, :]),
                            waits=[(sMM, sMM.n)], incs=[(sW, 16)])
                    wGr = sW.n
                    for cx in (0, 512):
                        sch.op("tensor", lambda e, cx=cx: e.matmul(
                            psT2[:, :], ones1r[0:1, 0:128],
                            scB[0:1, cx: cx + 512],
                            start=True, stop=True),
                            waits=[(sW, wGr), (sEv, sEv.n)], incs=[(sMM, 1)])
                        sch.op("vector", lambda e, dstb=dstb, cx=cx: e.tensor_copy(
                            dstb[:, cx:cx + 512], psT2[:, :]),
                            waits=[(sMM, sMM.n)], incs=[(sEv, 1)])
                gbcEv = sEv.n
                xres_rel = 0
                vall_rel = [0, 0]
                ps2_rel = [0, gbcEv]
                psT2_rel = [gbcEv]
                x1tm_rel = [0]
                xt8_relO = [0]
                xt8_relS = [0]
                psG_rel = [0]
                for pi in range(4):
                    off = pi * 512
                    pb = pi % 2
                    vbase = 0
                    sch.op("sync", lambda e, off=off: e.dma_start(
                        out=xres[:, :].rearrange("p (f s) -> p f s", f=NFC),
                        in_=xcf3[:, :, off:off + 512]),
                        waits=[(sEv, max(vall_rel[0], attnEv))], incs=[(sLd, 16)])
                    ldX = sLd.n
                    for fco in range(NFC):
                        ps = psA2 if fco % 2 == 0 else psB2
                        for kc in range(NFC):
                            w0 = []
                            if kc == 0:
                                w0 = [(sEv, ps2_rel[fco % 2])]
                                if fco == 0 and pi == 0:
                                    w0 += [(sW, wWo)]
                            sch.op("tensor", lambda e, ps=ps, kc=kc, fco=fco, off=off: e.matmul(
                                ps[:, :],
                                wOb[:, kc * 1024 + fco * 128: kc * 1024 + fco * 128 + 128],
                                bigC[:, kc * NQ + off: kc * NQ + off + 512],
                                start=(kc == 0), stop=(kc == NFC - 1)),
                                waits=w0,
                                incs=([(sMM, 1)] if kc == NFC - 1 else []))
                        sch.op("vector", lambda e, ps=ps, fco=fco: e.tensor_add(
                            xres[:, fco * 512:(fco + 1) * 512],
                            xres[:, fco * 512:(fco + 1) * 512], ps[:, :]),
                            waits=[(sMM, sMM.n), (sLd, ldX), (sEv, vall_rel[0])],
                            incs=[(sEv, 1)])
                        sch.op("vector", lambda e, fco=fco: e.tensor_add(
                            xres[:, fco * 512:(fco + 1) * 512],
                            xres[:, fco * 512:(fco + 1) * 512],
                            cBo[:, fco:fco + 1].broadcast_to((128, 512))),
                            incs=[(sEv, 1)])
                        ps2_rel[fco % 2] = sEv.n
                    xres_rel = sEv.n
                    voDone = sEv.n
                    sch.enabled = PHASES_B >= 2 and PH2SUB >= 3
                    for ti in range(4):
                        t = pi * 4 + ti
                        loff = ti * 128
                        for fcx in range(NFC):
                            w0 = []
                            if fcx % 4 == 0:
                                w0 = [(sEv, psT2_rel[0])]
                                if fcx == 0 and ti == 0:
                                    w0.append((sEv, voDone))
                            sch.op("tensor", lambda e, fcx=fcx, loff=loff: e.transpose(
                                psT2[:, (fcx % 4) * 128:(fcx % 4) * 128 + 128],
                                xres[:, fcx * 512 + loff: fcx * 512 + loff + 128],
                                cIdf[:, :]),
                                waits=w0,
                                incs=([(sMM, 1)] if fcx % 4 == 3 else []))
                            if fcx % 4 == 3:
                                sch.op("vector", lambda e, fcx=fcx: e.tensor_copy(
                                    vtm[:, (fcx - 3) * 128:(fcx + 1) * 128], psT2[:, :]),
                                    waits=[(sMM, sMM.n)], incs=[(sEv, 1)])
                                psT2_rel[0] = sEv.n
                        tEv = sEv.n
                        sch.op("vector", lambda e: e.reduce_sum(
                            scB[:, 0:1], vtm[:, :], axis=AX.X),
                            waits=[(sEv, tEv)], incs=[(sEv, 1)])
                        sch.op("vector", lambda e: e.tensor_scalar(
                            scB[:, 8:9], scB[:, 0:1], -1.0 / D, None, ALU.mult),
                            waits=[(sEv, sEv.n)], incs=[(sEv, 1)])
                        sch.op("vector", lambda e: e.tensor_add(
                            x1tm[:, :], vtm[:, :], scB[:, 8:9].broadcast_to((128, D))),
                            waits=[(sEv, sEv.n), (sMM, x1tm_rel[0])], incs=[(sEv, 1)])
                        sch.op("vector", lambda e: e.tensor_mul(
                            vsq[:, :], x1tm[:, :], x1tm[:, :]),
                            waits=[(sEv, sEv.n), (sOut, xt8_relO[0]),
                                   (sSc, xt8_relS[0])], incs=[(sEv, 1)])
                        sch.op("vector", lambda e: e.reduce_sum(
                            scB[:, 1:2], vsq[:, :], axis=AX.X),
                            waits=[(sEv, sEv.n)], incs=[(sEv, 1)])
                        sch.op("vector", lambda e: e.tensor_scalar(
                            scB[:, 4:5], scB[:, 1:2], 1.0 / D, EPS, ALU.mult, ALU.add),
                            waits=[(sEv, sEv.n)], incs=[(sEv, 1)])
                        sch.op("scalar", lambda e: e.activation(
                            scB[:, 7:8], scB[:, 4:5], AF.Ln),
                            waits=[(sEv, sEv.n)], incs=[(sSc, 1)])
                        sch.op("scalar", lambda e: e.activation(
                            scB[:, 3:4], scB[:, 7:8], AF.Exp, scale=-0.5),
                            waits=[(sSc, sSc.n)], incs=[(sSc, 1)])
                        sch.op("vector", None, waits=[(sSc, sSc.n)], incs=[(sEv, 1)])
                        sch.op("vector", lambda e: e.tensor_mul(
                            x1tm[:, :], x1tm[:, :],
                            scB[:, 3:4].broadcast_to((128, D))),
                            waits=[(sEv, sEv.n)], incs=[(sEv, 1)])
                        sch.op("vector", lambda e: e.tensor_mul(
                            x1tm[:, :], x1tm[:, :], g1bc[:, :]),
                            waits=[(sEv, sEv.n)], incs=[(sEv, 1)])
                        sch.op("vector", lambda e: e.tensor_add(
                            x1tm[:, :], x1tm[:, :], b1bc[:, :]),
                            waits=[(sEv, sEv.n)], incs=[(sEv, 1)])
                        apEv = sEv.n
                        for fcx in range(NFC):
                            w0 = []
                            if fcx % 4 == 0:
                                w0 = [(sEv, psT2_rel[0]), (sOut, xt8_relO[0]),
                                      (sSc, xt8_relS[0])]
                                if fcx == 0:
                                    w0.append((sEv, apEv))
                            sch.op("tensor", lambda e, fcx=fcx: e.transpose(
                                psT2[:, (fcx % 4) * 128:(fcx % 4) * 128 + 128],
                                x1tm[:, fcx * 128:(fcx + 1) * 128], cIdf[:, :]),
                                waits=w0,
                                incs=([(sMM, 1)] if fcx % 4 == 3 else []))
                            if fcx % 4 == 3:
                                for f2 in range(fcx - 3, fcx + 1):
                                    sch.op("vector", lambda e, f2=f2: e.tensor_copy(
                                        xt8[:, f2 * 128:(f2 + 1) * 128],
                                        psT2[:, (f2 % 4) * 128:(f2 % 4) * 128 + 128]),
                                        waits=([(sMM, sMM.n)] if f2 % 4 == 0 else []),
                                        incs=([(sEv, 1)] if f2 % 4 == 3 else []))
                                psT2_rel[0] = sEv.n
                        x1tm_rel[0] = sMM.n
                        xtEv = sEv.n
                        for fcx in range(NFC):
                            sch.op("scalar", lambda e, fcx=fcx, t=t: e.activation(
                                xi_bf[:, fcx * NQ + t * 128: fcx * NQ + t * 128 + 128],
                                xt8[:, fcx * 128:(fcx + 1) * 128], AF.Identity),
                                waits=([(sEv, xtEv)] if fcx == 0 else []),
                                incs=([(sSc, 1)] if fcx == NFC - 1 else []))
                        xt8_relS[0] = sSc.n
                        sch.op("sync", lambda e, t=t: e.dma_start(
                            out=xi_scr3[:, :, t * 128:(t + 1) * 128],
                            in_=xt8[:, :].rearrange("p (f s) -> p f s", f=NFC)),
                            waits=[(sEv, xtEv)], incs=[(sOut, 16)])
                        xt8_relO[0] = sOut.n
                        if ti == 3:
                            vall_rel[0] = xtEv
                    sch.enabled = PHASES_B >= 2 and PH2SUB >= 4
                    scXi = sSc.n
                    for kc in range(NFC):
                        w0 = []
                        if kc == 0:
                            w0 = [(sSc, scXi), (sEv, psG_rel[0])]
                        sch.op("tensor", lambda e, kc=kc, off=off: e.matmul(
                            psG[:, :],
                            gwb[:, kc * 128: kc * 128 + NEXP],
                            xi_bf[:, kc * NQ + off: kc * NQ + off + 512],
                            start=(kc == 0), stop=(kc == NFC - 1)),
                            waits=w0,
                            incs=([(sMM, 1)] if kc == NFC - 1 else []))
                    sch.op("vector", lambda e: e.tensor_add(
                        scB[0:NEXP, 512:1024], psG[:, :],
                        cGb[:, 0:1].broadcast_to((NEXP, 512))),
                        waits=[(sMM, sMM.n)], incs=[(sEv, 1)])
                    psG_rel[0] = sEv.n
                    for blk_ in range(4):
                        sch.op("vector", lambda e, blk_=blk_: e.tensor_copy(
                            scP[0:NEXP, 0:128],
                            scB[0:NEXP, 512 + blk_ * 128: 512 + (blk_ + 1) * 128]),
                            waits=[(sEv, sEv.n), (sMM, sMM.n)], incs=[(sEv, 1)])
                        sch.op("tensor", lambda e: e.transpose(
                            psGT[:, :], scP[:, 0:128], cIdf[:, :]),
                            waits=[(sEv, sEv.n)], incs=[(sMM, 1)])
                        sch.op("vector", lambda e: e.tensor_copy(scP[:, 128:136], psGT[:, 0:8]),
                               waits=[(sMM, sMM.n)], incs=[(sEv, 1)])
                        sch.op("vector", lambda e: e.reduce_max(scP[:, 136:137], scP[:, 128:136],
                                                                axis=AX.X),
                               waits=[(sEv, sEv.n)], incs=[(sEv, 1)])
                        sch.op("vector", lambda e: e.tensor_scalar(
                            scP[:, 137:138], scP[:, 136:137], -1.0, None, ALU.mult),
                            waits=[(sEv, sEv.n)], incs=[(sEv, 1)])
                        sch.op("scalar", lambda e: e.activation(
                            scP[:, 144:152], scP[:, 128:136], AF.Exp, bias=scP[:, 137:138]),
                            waits=[(sEv, sEv.n)], incs=[(sSc, 1)])
                        sch.op("vector", lambda e: e.reduce_sum(scP[:, 138:139], scP[:, 144:152],
                                                                axis=AX.X),
                               waits=[(sSc, sSc.n)], incs=[(sEv, 1)])
                        sch.op("vector", lambda e: e.reciprocal(scP[:, 139:140], scP[:, 138:139]),
                               waits=[(sEv, sEv.n)], incs=[(sEv, 1)])
                        sch.op("vector", lambda e: e.tensor_mul(
                            scP[:, 144:152], scP[:, 144:152],
                            scP[:, 139:140].broadcast_to((128, 8))),
                            waits=[(sEv, sEv.n)], incs=[(sEv, 1)])
                        sch.op("vector", lambda e: e.tensor_copy(scP[:, 0:8], scP[:, 144:152]),
                               waits=[(sEv, sEv.n)], incs=[(sEv, 1)])
                        sch.op("tensor", lambda e: e.transpose(
                            psGT[:, :], scP[:, 0:128], cIdf[:, :]),
                            waits=[(sEv, sEv.n)], incs=[(sMM, 1)])
                        sch.op("vector", lambda e, blk_=blk_, off=off: e.tensor_copy(
                            probs[0:NEXP, off + blk_ * 128: off + blk_ * 128 + 128],
                            psGT[0:NEXP, 0:128]),
                            waits=[(sMM, sMM.n)], incs=[(sEv, 1)])
                    sch.enabled = PHASES_B >= 2
                xiEv = sEv.n
                xiMM = sMM.n
                xiSc = sSc.n
                xiOut = sOut.n

        with contextlib.ExitStack() as stP4:
            sch.enabled = PHASES_B >= 3
            w1q = stP4.enter_context(nc.sbuf_tensor("w1q", [128, 2 * 8192], BF16))
            w2q = stP4.enter_context(nc.sbuf_tensor("w2q", [128, 2 * 8192], BF16))
            h_all = stP4.enter_context(nc.sbuf_tensor("h_all", [128, 8 * 512], BF16))
            hbx = stP4.enter_context(nc.sbuf_tensor("hbx", [128, 512], F32))
            pb_all = stP4.enter_context(nc.sbuf_tensor("pb_all", [128, NFC * 256], F32))
            b2w = stP4.enter_context(nc.sbuf_tensor("b2w", [NEXP, D], F32))
            tmpb = stP4.enter_context(nc.sbuf_tensor("tmpb", [128, 256], F32))
            psH = stP4.enter_context(nc.psum_tensor("psH", [128, 2 * 512], F32))
            psY = stP4.enter_context(nc.psum_tensor("psY", [128, 8 * 256], F32))
            psPb = stP4.enter_context(nc.psum_tensor("psPb", [128, 256], F32))
            bnd2 = [(sMM, xiMM), (sEv, xiEv), (sSc, xiSc),
                    (sOut, xiOut), (sLd, sLd.n)]
            cSel = stP4.enter_context(nc.sbuf_tensor("cSel", [NEXP, NEXP * 128], F32))
            sch.op("gpsimd", lambda e: e.dma_start(out=b2w[:, :], in_=ib2f[:, :]),
                   waits=bnd2, incs=[(sW, 16)])
            sch.op("gpsimd", lambda e: e.dma_start(out=cSel[:, :], in_=isel[:, :]),
                   incs=[(sW, 16)])
            wB2 = sW.n
            psY_rel = [0]
            for g8 in range(8):
                for fh in range(2):
                    for f4 in range(4):
                        fco = fh * 4 + f4
                        w0 = [(sW, wB2)] if (g8 == 0 and fco == 0) else []
                        if f4 == 0:
                            w0.append((sEv, psY_rel[0]))
                        sch.op("tensor", lambda e, g8=g8, fco=fco, f4=f4: e.matmul(
                            psY[:, f4 * 512: f4 * 512 + 256],
                            b2w[:, fco * 128:(fco + 1) * 128],
                            probs[:, g8 * 256:(g8 + 1) * 256],
                            start=True, stop=True),
                            waits=w0, incs=[(sMM, 1)])
                    mmB = sMM.n
                    for f4 in range(4):
                        fco = fh * 4 + f4
                        sch.op("vector", lambda e, g8=g8, fco=fco, f4=f4: e.tensor_copy(
                            acc[:, fco * NQ + g8 * 256: fco * NQ + g8 * 256 + 256],
                            psY[:, f4 * 512: f4 * 512 + 256]),
                            waits=([(sMM, mmB)] if f4 == 0 else []),
                            incs=([(sEv, 1)] if f4 == 3 else []))
                    psY_rel[0] = sEv.n
            w1_rel = [0, 0]
            w2_rel = [0, 0]
            pb_rel = [0]
            psH_rel = [0, 0]
            h_rel = [0]
            for ex in range(NEXP):
                for g8 in range(8):
                    w0 = [(sEv, sEv.n if g8 > 0 else max(pb_rel[0], psY_rel[0]))]
                    sch.op("tensor", lambda e, ex=ex, g8=g8: e.matmul(
                        psPb[:, :], cSel[:, ex * 128:(ex + 1) * 128],
                        probs[:, g8 * 256:(g8 + 1) * 256],
                        start=True, stop=True),
                        waits=w0, incs=[(sMM, 1)])
                    sch.op("vector", lambda e, g8=g8: e.tensor_copy(
                        pb_all[:, g8 * 256:(g8 + 1) * 256], psPb[:, :]),
                        waits=[(sMM, sMM.n)], incs=[(sEv, 1)])
                pbEv = sEv.n
                for blk in range(4):
                    slot = blk % 2
                    sch.op("gpsimd", lambda e, ex=ex, blk=blk, slot=slot: e.dma_start(
                        out=w1q[:, slot * 8192: slot * 8192 + 8192].rearrange(
                            "p (f c) -> p f c", f=NFC),
                        in_=w1i[ex].rearrange("(f p) c -> p f c", p=128)[
                            :, :, blk * 1024:(blk + 1) * 1024]),
                        waits=[(sMM, w1_rel[slot])], incs=[(sW, 16)])
                    sch.op("gpsimd", lambda e, ex=ex, blk=blk, slot=slot: e.dma_start(
                        out=w2q[:, slot * 8192: slot * 8192 + 8192].rearrange(
                            "p (d o) -> p d o", d=NFC),
                        in_=w2i[ex, blk * 1024:(blk + 1) * 1024, :].rearrange(
                            "(d p) o -> p d o", p=128)),
                        waits=[(sMM, w2_rel[slot])], incs=[(sW, 16)])
                    wQ = sW.n
                    for gp in range(4):
                        for d8 in range(8):
                            hs = d8 % 2
                            for fc in range(NFC):
                                w0 = []
                                if fc == 0:
                                    w0 = [(sSc, psH_rel[hs])]
                                    if d8 == 0 and gp == 0:
                                        w0.append((sW, wQ))
                                        if blk == 0 and ex == 0:
                                            w0 += [(sSc, xiSc), (sEv, xiEv)]
                                sch.op("tensor", lambda e, slot=slot, fc=fc, d8=d8,
                                       gp=gp, hs=hs: e.matmul(
                                           psH[:, hs * 512:(hs + 1) * 512],
                                           w1q[:, slot * 8192 + fc * 1024 + d8 * 128:
                                               slot * 8192 + fc * 1024 + d8 * 128 + 128],
                                           xi_bf[:, fc * NQ + gp * 512:
                                                 fc * NQ + gp * 512 + 512],
                                           start=(fc == 0), stop=(fc == NFC - 1)),
                                        waits=w0,
                                        incs=([(sMM, 1)] if fc == NFC - 1 else []))
                            bcol = ex * 32 + blk * 8 + d8
                            sch.op("scalar", lambda e, hs=hs, bcol=bcol: e.activation(
                                hbx[:, :], psH[:, hs * 512:(hs + 1) * 512],
                                AF.Identity, bias=cB1[:, bcol:bcol + 1]),
                                waits=[(sMM, sMM.n)], incs=[(sSc, 1)])
                            psH_rel[hs] = sSc.n
                            sch.op("scalar", lambda e, d8=d8: e.activation(
                                h_all[:, d8 * 512:(d8 + 1) * 512], hbx[:, :], AF.Relu),
                                waits=[(sMM, h_rel[0]), (sSc, sSc.n)], incs=[(sSc, 1)])
                        scH = sSc.n
                        for g8sub in range(2):
                            g8 = gp * 2 + g8sub
                            for fh in range(2):
                                for d8 in range(8):
                                    for f4 in range(4):
                                        fco = fh * 4 + f4
                                        w0 = []
                                        if d8 == 0 and f4 == 0:
                                            w0 = [(sSc, scH), (sEv, psY_rel[0])]
                                        sch.op("tensor", lambda e, slot=slot, d8=d8,
                                               fco=fco, f4=f4, g8sub=g8sub: e.matmul(
                                                   psY[:, f4 * 512: f4 * 512 + 256],
                                                   w2q[:, slot * 8192 + d8 * 1024 + fco * 128:
                                                       slot * 8192 + d8 * 1024 + fco * 128 + 128],
                                                   h_all[:, d8 * 512 + g8sub * 256:
                                                         d8 * 512 + g8sub * 256 + 256],
                                                   start=(d8 == 0), stop=(d8 == 7)),
                                                waits=w0,
                                                incs=([(sMM, 1)] if d8 == 7 and
                                                      f4 == 3 else []))
                                mmY = sMM.n
                                if g8sub == 1 and fh == 1:
                                    h_rel[0] = mmY
                                for f4 in range(4):
                                    fco = fh * 4 + f4
                                    sch.op("vector", lambda e, f4=f4, g8=g8: e.tensor_mul(
                                        tmpb[:, :], psY[:, f4 * 512: f4 * 512 + 256],
                                        pb_all[:, g8 * 256:(g8 + 1) * 256]),
                                        waits=([(sMM, mmY), (sEv, pbEv)]
                                               if f4 == 0 else []) + [(sEv, sEv.n)],
                                        incs=[(sEv, 1)])
                                    sch.op("vector", lambda e, fco=fco, g8=g8: e.tensor_add(
                                        acc[:, fco * NQ + g8 * 256: fco * NQ + g8 * 256 + 256],
                                        acc[:, fco * NQ + g8 * 256: fco * NQ + g8 * 256 + 256],
                                        tmpb[:, :]),
                                        waits=[(sEv, sEv.n)], incs=[(sEv, 1)])
                                psY_rel[0] = sEv.n
                    w1_rel[slot] = sMM.n
                    w2_rel[slot] = sMM.n
                pb_rel[0] = sEv.n
            moeMM = sMM.n
            moeEv = sEv.n
            moeSc = sSc.n

        with contextlib.ExitStack() as stP5:
            sch.enabled = PHASES_B >= 4
            wWb = stP5.enter_context(nc.sbuf_tensor("wWb", [128, NFC * 1024], BF16))
            xiL = stP5.enter_context(nc.sbuf_tensor("xiL", [128, NFC * 512], F32))
            x1L = stP5.enter_context(nc.sbuf_tensor("x1L", [128, NFC * 512], F32))
            vall2 = stP5.enter_context(nc.sbuf_tensor("vall2", [128, NFC * 512], F32))
            utm = stP5.enter_context(nc.sbuf_tensor("utm", [128, 1024], F32))
            usq = stP5.enter_context(nc.sbuf_tensor("usq", [128, 1024], F32))
            wtm = stP5.enter_context(nc.sbuf_tensor("wtm", [128, 1024], F32))
            x1tm2 = stP5.enter_context(nc.sbuf_tensor("x1tm2", [128, 1024], F32))
            xt8b = stP5.enter_context(nc.sbuf_tensor("xt8b", [128, 1024], F32))
            g2bc = stP5.enter_context(nc.sbuf_tensor("g2bc", [128, 1024], F32))
            b2bc = stP5.enter_context(nc.sbuf_tensor("b2bc", [128, 1024], F32))
            vbf = stP5.enter_context(nc.sbuf_tensor("vbf", [128, 2 * 512], BF16))
            psA4 = stP5.enter_context(nc.psum_tensor("psA4", [128, 512], F32))
            psB4 = stP5.enter_context(nc.psum_tensor("psB4", [128, 512], F32))
            psT4 = stP5.enter_context(nc.psum_tensor("psT4", [128, 512], F32))
            bnd3 = [(sMM, moeMM), (sEv, moeEv), (sSc, moeSc), (sW, sW.n)]
            sch.op("gpsimd", lambda e: e.dma_start(
                out=wWb[:, :].rearrange("p (k o) -> p k o", k=NFC),
                in_=WwTi.rearrange("(k p) o -> p k o", p=128)),
                waits=bnd3, incs=[(sW, 16)])
            sch.op("gpsimd", lambda e: e.dma_start(out=scB[0:1, 0:1024], in_=iG2[:, :]),
                   incs=[(sW, 16)])
            wW2 = sW.n
            for gi, dstb in ((0, g2bc), (1, b2bc)):
                if gi == 1:
                    sch.op("gpsimd", lambda e: e.dma_start(
                        out=scB[0:1, 0:1024], in_=iB2r[:, :]),
                        waits=[(sMM, sMM.n)], incs=[(sW, 16)])
                wGr2 = sW.n
                for cx in (0, 512):
                    sch.op("tensor", lambda e, cx=cx: e.matmul(
                        psT4[:, :], ones1r[0:1, 0:128],
                        scB[0:1, cx: cx + 512],
                        start=True, stop=True),
                        waits=[(sW, wGr2), (sEv, sEv.n)], incs=[(sMM, 1)])
                    sch.op("vector", lambda e, dstb=dstb, cx=cx: e.tensor_copy(
                        dstb[:, cx:cx + 512], psT4[:, :]),
                        waits=[(sMM, sMM.n)], incs=[(sEv, 1)])
            gbc2 = sEv.n
            psT4_rel = [gbc2]
            utm_rel = [0]
            vall2_rel = [0]
            xt8b_relO = [0]
            xt8b_relS = [0]
            ps4_rel = [0, 0]
            vbf_rel = [0, 0]
            x2scDone = 0
            for pi in range(4):
                off = pi * 512
                sch.op("sync", lambda e, off=off: e.dma_start(
                    out=xiL[:, :].rearrange("p (f s) -> p f s", f=NFC),
                    in_=xi_scr3[:, :, off:off + 512]),
                    waits=[(sEv, max(vall2_rel[0], gbc2))], incs=[(sLd, 16)])
                sch.op("sync", lambda e, off=off: e.dma_start(
                    out=x1L[:, :].rearrange("p (f s) -> p f s", f=NFC),
                    in_=x1f3[:, :, off:off + 512]),
                    incs=[(sLd, 16)])
                ldP = sLd.n
                for fc in range(NFC):
                    sch.op("vector", lambda e, fc=fc, off=off: e.tensor_add(
                        vall2[:, fc * 512:(fc + 1) * 512],
                        xiL[:, fc * 512:(fc + 1) * 512],
                        acc[:, fc * NQ + off: fc * NQ + off + 512]),
                        waits=([(sLd, ldP), (sEv, vall2_rel[0])] if fc == 0 else []),
                        incs=([(sEv, 1)] if fc == NFC - 1 else []))
                vuDone = sEv.n
                for ti in range(4):
                    t = pi * 4 + ti
                    loff = ti * 128
                    for fcx in range(NFC):
                        w0 = []
                        if fcx % 4 == 0:
                            w0 = [(sEv, psT4_rel[0])]
                            if fcx == 0 and ti == 0:
                                w0.append((sEv, vuDone))
                            if fcx == 0:
                                w0.append((sMM, utm_rel[0]))
                        sch.op("tensor", lambda e, fcx=fcx, loff=loff: e.transpose(
                            psT4[:, (fcx % 4) * 128:(fcx % 4) * 128 + 128],
                            vall2[:, fcx * 512 + loff: fcx * 512 + loff + 128],
                            cIdf[:, :]),
                            waits=w0,
                            incs=([(sMM, 1)] if fcx % 4 == 3 else []))
                        if fcx % 4 == 3:
                            sch.op("vector", lambda e, fcx=fcx: e.tensor_copy(
                                utm[:, (fcx - 3) * 128:(fcx + 1) * 128], psT4[:, :]),
                                waits=[(sMM, sMM.n)], incs=[(sEv, 1)])
                            psT4_rel[0] = sEv.n
                    tEv = sEv.n
                    sch.op("vector", lambda e: e.reduce_sum(
                        scB[:, 0:1], utm[:, :], axis=AX.X),
                        waits=[(sEv, tEv)], incs=[(sEv, 1)])
                    sch.op("vector", lambda e: e.tensor_scalar(
                        scB[:, 8:9], scB[:, 0:1], -1.0 / D, None, ALU.mult),
                        waits=[(sEv, sEv.n)], incs=[(sEv, 1)])
                    sch.op("vector", lambda e: e.tensor_add(
                        wtm[:, :], utm[:, :], scB[:, 8:9].broadcast_to((128, D))),
                        waits=[(sEv, sEv.n)], incs=[(sEv, 1)])
                    sch.op("vector", lambda e: e.tensor_mul(
                        usq[:, :], wtm[:, :], wtm[:, :]),
                        waits=[(sEv, sEv.n)], incs=[(sEv, 1)])
                    sch.op("vector", lambda e: e.reduce_sum(
                        scB[:, 1:2], usq[:, :], axis=AX.X),
                        waits=[(sEv, sEv.n)], incs=[(sEv, 1)])
                    sch.op("vector", lambda e: e.tensor_scalar(
                        scB[:, 4:5], scB[:, 1:2], 1.0 / D, EPS, ALU.mult, ALU.add),
                        waits=[(sEv, sEv.n)], incs=[(sEv, 1)])
                    sch.op("scalar", lambda e: e.activation(
                        scB[:, 7:8], scB[:, 4:5], AF.Ln),
                        waits=[(sEv, sEv.n)], incs=[(sSc, 1)])
                    sch.op("scalar", lambda e: e.activation(
                        scB[:, 3:4], scB[:, 7:8], AF.Exp, scale=-0.5),
                        waits=[(sSc, sSc.n)], incs=[(sSc, 1)])
                    sch.op("vector", None, waits=[(sSc, sSc.n)], incs=[(sEv, 1)])
                    sch.op("vector", lambda e: e.tensor_mul(
                        wtm[:, :], wtm[:, :], scB[:, 3:4].broadcast_to((128, D))),
                        waits=[(sEv, sEv.n)], incs=[(sEv, 1)])
                    sch.op("vector", lambda e: e.tensor_mul(
                        wtm[:, :], wtm[:, :], g2bc[:, :]),
                        waits=[(sEv, sEv.n)], incs=[(sEv, 1)])
                    sch.op("vector", lambda e: e.tensor_add(
                        wtm[:, :], wtm[:, :], b2bc[:, :]),
                        waits=[(sEv, sEv.n)], incs=[(sEv, 1)])
                    ffnEv = sEv.n
                    for fcx in range(NFC):
                        w0 = []
                        if fcx % 4 == 0:
                            w0 = [(sEv, psT4_rel[0])]
                        sch.op("tensor", lambda e, fcx=fcx, loff=loff: e.transpose(
                            psT4[:, (fcx % 4) * 128:(fcx % 4) * 128 + 128],
                            x1L[:, fcx * 512 + loff: fcx * 512 + loff + 128],
                            cIdf[:, :]),
                            waits=w0,
                            incs=([(sMM, 1)] if fcx % 4 == 3 else []))
                        if fcx % 4 == 3:
                            sch.op("vector", lambda e, fcx=fcx: e.tensor_copy(
                                x1tm2[:, (fcx - 3) * 128:(fcx + 1) * 128], psT4[:, :]),
                                waits=[(sMM, sMM.n)], incs=[(sEv, 1)])
                            psT4_rel[0] = sEv.n
                    sch.op("vector", lambda e: e.tensor_add(
                        utm[:, :], x1tm2[:, :], wtm[:, :]),
                        waits=[(sEv, sEv.n), (sEv, ffnEv)], incs=[(sEv, 1)])
                    sch.op("vector", lambda e: e.reduce_sum(
                        scB[:, 0:1], utm[:, :], axis=AX.X),
                        waits=[(sEv, sEv.n)], incs=[(sEv, 1)])
                    sch.op("vector", lambda e: e.tensor_scalar(
                        scB[:, 8:9], scB[:, 0:1], -1.0 / D, None, ALU.mult),
                        waits=[(sEv, sEv.n)], incs=[(sEv, 1)])
                    sch.op("vector", lambda e: e.tensor_add(
                        utm[:, :], utm[:, :], scB[:, 8:9].broadcast_to((128, D))),
                        waits=[(sEv, sEv.n)], incs=[(sEv, 1)])
                    sch.op("vector", lambda e: e.tensor_mul(
                        usq[:, :], utm[:, :], utm[:, :]),
                        waits=[(sEv, sEv.n)], incs=[(sEv, 1)])
                    sch.op("vector", lambda e: e.reduce_sum(
                        scB[:, 1:2], usq[:, :], axis=AX.X),
                        waits=[(sEv, sEv.n)], incs=[(sEv, 1)])
                    sch.op("vector", lambda e: e.tensor_scalar(
                        scB[:, 4:5], scB[:, 1:2], 1.0 / D, EPS, ALU.mult, ALU.add),
                        waits=[(sEv, sEv.n)], incs=[(sEv, 1)])
                    sch.op("scalar", lambda e: e.activation(
                        scB[:, 7:8], scB[:, 4:5], AF.Ln),
                        waits=[(sEv, sEv.n)], incs=[(sSc, 1)])
                    sch.op("scalar", lambda e: e.activation(
                        scB[:, 3:4], scB[:, 7:8], AF.Exp, scale=-0.5),
                        waits=[(sSc, sSc.n)], incs=[(sSc, 1)])
                    sch.op("vector", None, waits=[(sSc, sSc.n)], incs=[(sEv, 1)])
                    sch.op("vector", lambda e: e.tensor_mul(
                        utm[:, :], utm[:, :], scB[:, 3:4].broadcast_to((128, D))),
                        waits=[(sEv, sEv.n)], incs=[(sEv, 1)])
                    x2Ev = sEv.n
                    for fcx in range(NFC):
                        w0 = []
                        if fcx % 4 == 0:
                            w0 = [(sEv, psT4_rel[0]), (sOut, xt8b_relO[0]),
                                  (sSc, xt8b_relS[0])]
                            if fcx == 0:
                                w0.append((sEv, x2Ev))
                        sch.op("tensor", lambda e, fcx=fcx: e.transpose(
                            psT4[:, (fcx % 4) * 128:(fcx % 4) * 128 + 128],
                            utm[:, fcx * 128:(fcx + 1) * 128], cIdf[:, :]),
                            waits=w0,
                            incs=([(sMM, 1)] if fcx % 4 == 3 else []))
                        if fcx % 4 == 3:
                            for f2 in range(fcx - 3, fcx + 1):
                                sch.op("vector", lambda e, f2=f2: e.tensor_copy(
                                    xt8b[:, f2 * 128:(f2 + 1) * 128],
                                    psT4[:, (f2 % 4) * 128:(f2 % 4) * 128 + 128]),
                                    waits=([(sMM, sMM.n)] if f2 % 4 == 0 else []),
                                    incs=([(sEv, 1)] if f2 % 4 == 3 else []))
                            psT4_rel[0] = sEv.n
                    utm_rel[0] = sMM.n
                    xtEv2 = sEv.n
                    for fcx in range(NFC):
                        sch.op("scalar", lambda e, fcx=fcx, t=t: e.activation(
                            xi_bf[:, fcx * NQ + t * 128: fcx * NQ + t * 128 + 128],
                            xt8b[:, fcx * 128:(fcx + 1) * 128], AF.Identity),
                            waits=([(sEv, xtEv2)] if fcx == 0 else []),
                            incs=([(sSc, 1)] if fcx == NFC - 1 else []))
                    xt8b_relS[0] = sSc.n
                    sch.op("sync", lambda e, t=t: e.dma_start(
                        out=x2_out.rearrange("p (f s) -> p f s", f=NFC)[
                            :, :, t * 128:(t + 1) * 128],
                        in_=xt8b[:, :].rearrange("p (f s) -> p f s", f=NFC)),
                        waits=[(sEv, xtEv2)], incs=[(sOut, 16)])
                    xt8b_relO[0] = sOut.n
                    if ti == 3:
                        vall2_rel[0] = xtEv2
                        x2scDone = sSc.n
                for fco in range(NFC):
                    ps = psA4 if fco % 2 == 0 else psB4
                    vs = fco % 2
                    for kc in range(NFC):
                        w0 = []
                        if kc == 0:
                            w0 = [(sSc, ps4_rel[fco % 2]), (sSc, x2scDone)]
                            if fco == 0 and pi == 0:
                                w0.append((sW, wW2))
                        sch.op("tensor", lambda e, ps=ps, kc=kc, fco=fco, off=off: e.matmul(
                            ps[:, :],
                            wWb[:, kc * 1024 + fco * 128: kc * 1024 + fco * 128 + 128],
                            xi_bf[:, kc * NQ + off: kc * NQ + off + 512],
                            start=(kc == 0), stop=(kc == NFC - 1)),
                            waits=w0,
                            incs=([(sMM, 1)] if kc == NFC - 1 else []))
                    sch.op("scalar", lambda e, ps=ps, vs=vs: e.activation(
                        vbf[:, vs * 512:(vs + 1) * 512], ps[:, :], AF.Identity),
                        waits=[(sMM, sMM.n), (sOut, vbf_rel[vs])], incs=[(sSc, 1)])
                    ps4_rel[fco % 2] = sSc.n
                    sch.op("sync", lambda e, fco=fco, off=off, vs=vs: e.dma_start(
                        out=v_out[:, fco * NQ + off: fco * NQ + off + 512],
                        in_=vbf[:, vs * 512:(vs + 1) * 512]),
                        waits=[(sSc, sSc.n)], incs=[(sOut, 16)])
                    vbf_rel[vs] = sOut.n
            sch.op("sync", None, waits=[(sOut, sOut.n)])
            for En in ("vector", "tensor", "scalar", "gpsimd"):
                sch.op(En, None, waits=[(sOut, sOut.n)])

        with nc.Block("launchB") as blk:
            sch.emit(blk)
    _B_CACHE["nc"] = nc
    return nc


def run_B(inputs, resA):
    maps = host_prep_B(inputs, resA)
    nc = build_launch_B()
    res = run_bass_kernel_spmd(nc, maps, core_ids=list(range(8)))
    return res.results


LOC = 8320
NCH = 65
OWN0 = 49


def host_prep_C(inputs, resB):
    WrT = np.ascontiguousarray(np.asarray(inputs["L1_attn_W_read"], np.float32).T)
    wTt = np.ascontiguousarray(np.asarray(inputs["L1_ffn_weight"], np.float32).T)
    biasc = np.ascontiguousarray(
        np.asarray(inputs["L1_ffn_bias"], np.float32).reshape(NFC, 128).T)
    alphaf = np.asarray(inputs["L1_ffn_alpha"], np.float32).reshape(1, 1)
    logal = np.asarray(inputs["L1_attn_log_alpha"], np.float32).reshape(1, 1)
    decayi = np.asarray(inputs["L1_attn_decay"], np.float32).reshape(1, 1)
    tt = np.arange(128, dtype=np.float32)
    Erow = np.ascontiguousarray(tt.reshape(1, 128))
    Ewrow = np.ascontiguousarray((127.0 - tt).reshape(1, 128))
    cc_ = tt[:, None]
    kk_ = tt[None, :]
    causal = (cc_ > kk_).astype(np.float32)
    EM2 = np.ascontiguousarray((cc_ - 128.0) * causal)
    u16 = lambda a: np.asarray(a).view(np.uint16)
    x2full = np.zeros((2, S, D), np.float32)
    vfull = np.zeros((2, S, D), np.uint16)
    for c in range(8):
        s2, r = divmod(c, 4)
        own = np.asarray(own_token_positions(r))
        x2full[s2][own] = _unfm(np.asarray(resB[c]["x2_out"]), NQ)
        vfull[s2][own] = _unfm(u16(resB[c]["v_out"]), NQ)
    wkfull = np.zeros((2, S, D), np.float32)
    wkfull[:, 1:] = x2full[:, :-1]
    maps = []
    for c in range(8):
        s2, r = divmod(c, 4)
        pad = 6272 - r * 2048
        wkl = np.zeros((LOC, D), np.float32)
        wkl[pad:] = wkfull[s2, :(r + 1) * 2048]
        vl = np.zeros((LOC, D), np.uint16)
        vl[pad:] = vfull[s2, :(r + 1) * 2048]
        rko = np.ascontiguousarray(x2full[s2, r * 2048:(r + 1) * 2048])
        maps.append({
            "wks": _fm(wkl), "vsc": _fm(vl).view(BF), "rko": _fm(rko),
            "WrT": WrT, "wTt": wTt, "biasc": biasc,
            "alphaf": alphaf, "logal": logal, "decayi": decayi,
            "Erow": Erow, "Ewrow": Ewrow, "EM2": EM2, "Mmask": causal,
            "identf": np.eye(128, dtype=np.float32),
            "identb": np.eye(128, dtype=BF),
        })
    return maps


_C_CACHE = {}


def build_launch_C():
    if "nc" in _C_CACHE:
        return _C_CACHE["nc"]
    nc = bass.Bass("TRN2", target_bir_lowering=False)
    dt = nc.dram_tensor
    wks = dt("wks", [128, NFC * LOC], F32, kind="ExternalInput").ap()
    vsc = dt("vsc", [128, NFC * LOC], BF16, kind="ExternalInput").ap()
    rko = dt("rko", [128, NFC * NQ], F32, kind="ExternalInput").ap()
    WrTi = dt("WrT", [D, D], F32, kind="ExternalInput").ap()
    wTti = dt("wTt", [D, D], F32, kind="ExternalInput").ap()
    ibias = dt("biasc", [128, NFC], F32, kind="ExternalInput").ap()
    ialf = dt("alphaf", [1, 1], F32, kind="ExternalInput").ap()
    ilga = dt("logal", [1, 1], F32, kind="ExternalInput").ap()
    idec = dt("decayi", [1, 1], F32, kind="ExternalInput").ap()
    iErow = dt("Erow", [1, 128], F32, kind="ExternalInput").ap()
    iEwrow = dt("Ewrow", [1, 128], F32, kind="ExternalInput").ap()
    iEM2 = dt("EM2", [128, 128], F32, kind="ExternalInput").ap()
    iMask = dt("Mmask", [128, 128], F32, kind="ExternalInput").ap()
    iidf = dt("identf", [128, 128], F32, kind="ExternalInput").ap()
    iidb = dt("identb", [128, 128], BF16, kind="ExternalInput").ap()
    out_f = dt("out_f", [128, NFC * NQ], F32, kind="ExternalOutput").ap()
    wks3 = wks.rearrange("p (f s) -> p f s", f=NFC)
    vsc3 = vsc.rearrange("p (f s) -> p f s", f=NFC)
    rko3 = rko.rearrange("p (f s) -> p f s", f=NFC)
    out3 = out_f.rearrange("p (f s) -> p f s", f=NFC)

    sch = Sched()
    with contextlib.ExitStack() as stack:
        al = stack.enter_context
        sIn = Sem(al(nc.semaphore("sIn")))
        sW = Sem(al(nc.semaphore("sW")))
        sMM = Sem(al(nc.semaphore("sMM")))
        sEv = Sem(al(nc.semaphore("sEv")))
        sSc = Sem(al(nc.semaphore("sSc")))
        sLd = Sem(al(nc.semaphore("sLd")))
        sOut = Sem(al(nc.semaphore("sOut")))
        cIdf = al(nc.sbuf_tensor("cIdf", [128, 128], F32))
        cIdb = al(nc.sbuf_tensor("cIdb", [128, 128], BF16))
        cBb = al(nc.sbuf_tensor("cBb", [128, NFC], F32))
        cM2 = al(nc.sbuf_tensor("cM2", [128, 128], F32))
        cMsk = al(nc.sbuf_tensor("cMsk", [128, 128], F32))
        ones1r = al(nc.sbuf_tensor("ones1r", [1, 128], F32))
        sstC = al(nc.sbuf_tensor("sstC", [1, 32], F32))
        srow = al(nc.sbuf_tensor("srow", [1, 128], F32))
        scC = al(nc.sbuf_tensor("scC", [128, 8], F32))
        scB = al(nc.sbuf_tensor("scB", [128, 1024], F32))
        gpt8 = al(nc.sbuf_tensor("gpt8", [128, 1024], F32))
        gw8 = al(nc.sbuf_tensor("gw8", [128, 1024], BF16))
        readsfm = al(nc.sbuf_tensor("readsfm", [128, NFC * NQ], BF16))

        for (dstn, srcn) in ((cIdf, iidf), (cIdb, iidb), (cBb, ibias),
                             (cM2, iEM2), (cMsk, iMask)):
            sch.op("sync", lambda e, d=dstn, s2=srcn: e.dma_start(out=d[:, :], in_=s2[:, :]),
                   incs=[(sIn, 16)])
        sch.op("gpsimd", lambda e: e.dma_start(out=sstC[0:1, 0:1], in_=idec[:, :]),
               incs=[(sIn, 16)])
        sch.op("gpsimd", lambda e: e.dma_start(out=sstC[0:1, 1:2], in_=ilga[:, :]),
               incs=[(sIn, 16)])
        sch.op("gpsimd", lambda e: e.dma_start(out=sstC[0:1, 2:3], in_=ialf[:, :]),
               incs=[(sIn, 16)])
        sch.op("gpsimd", lambda e: e.dma_start(out=sstC[0:1, 16:16 + 128].rearrange(
            "o (a b) -> o a b", a=1)[:, 0, :] if False else sstC[0:1, 3:4], in_=idec[:, :]),
            incs=[(sIn, 16)])
        sch.op("gpsimd", lambda e: e.dma_start(out=srow[0:1, :], in_=iErow[:, :]),
               incs=[(sIn, 16)])
        c0 = sIn.n
        sch.op("vector", lambda e: e.memset(ones1r[:, :], 1.0),
               waits=[(sIn, c0)], incs=[(sEv, 1)])
        sch.op("scalar", lambda e: e.activation(sstC[0:1, 4:5], sstC[0:1, 0:1],
                                                AF.Exp, scale=-1.0),
               waits=[(sEv, sEv.n)], incs=[(sSc, 1)])
        sch.op("vector", lambda e: e.tensor_scalar(sstC[0:1, 5:6], sstC[0:1, 4:5],
                                                   1.0, None, ALU.add),
               waits=[(sSc, sSc.n)], incs=[(sEv, 1)])
        sch.op("scalar", lambda e: e.activation(sstC[0:1, 6:7], sstC[0:1, 5:6], AF.Ln),
               waits=[(sEv, sEv.n)], incs=[(sSc, 1)])
        sch.op("vector", lambda e: e.tensor_scalar(sstC[0:1, 7:8], sstC[0:1, 6:7],
                                                   -1.0, None, ALU.mult),
               waits=[(sSc, sSc.n)], incs=[(sEv, 1)])
        sch.op("scalar", lambda e: e.activation(sstC[0:1, 8:9], sstC[0:1, 1:2], AF.Exp),
               waits=[(sEv, sEv.n)], incs=[(sSc, 1)])
        sch.op("vector", lambda e: e.tensor_scalar(sstC[0:1, 9:10], sstC[0:1, 7:8],
                                                   128.0, None, ALU.mult),
               waits=[(sSc, sSc.n)], incs=[(sEv, 1)])
        sch.op("scalar", lambda e: e.activation(sstC[0:1, 10:11], sstC[0:1, 9:10], AF.Exp),
               waits=[(sEv, sEv.n)], incs=[(sSc, 1)])
        sch.op("vector", lambda e: e.tensor_copy(sstC[0:1, 12:13], sstC[0:1, 7:8]),
               waits=[(sSc, sSc.n)], incs=[(sEv, 1)])
        sch.op("vector", lambda e: e.tensor_copy(sstC[0:1, 13:14], sstC[0:1, 8:9]),
               waits=[(sEv, sEv.n)], incs=[(sEv, 1)])
        sch.op("vector", lambda e: e.tensor_copy(sstC[0:1, 14:15], sstC[0:1, 10:11]),
               waits=[(sEv, sEv.n)], incs=[(sEv, 1)])
        sch.op("vector", lambda e: e.tensor_copy(sstC[0:1, 15:16], sstC[0:1, 2:3]),
               waits=[(sEv, sEv.n)], incs=[(sEv, 1)])
        packEv = sEv.n
        with contextlib.ExitStack() as stC0:
            psB0 = stC0.enter_context(nc.psum_tensor("psB0", [128, 128], F32))
            sch.op("tensor", lambda e: e.matmul(
                psB0[:, 0:4], ones1r[0:1, 0:128], sstC[0:1, 12:16],
                start=True, stop=True), waits=[(sEv, packEv)], incs=[(sMM, 1)])
            sch.op("vector", lambda e: e.tensor_copy(scC[:, 0:4], psB0[:, 0:4]),
                   waits=[(sMM, sMM.n)], incs=[(sEv, 1)])
            for (srci, dsti, dty) in ((iErow, gpt8, 0), (iEwrow, gw8, 1)):
                sch.op("gpsimd", lambda e, srci=srci: e.dma_start(
                    out=srow[0:1, :], in_=srci[:, :]),
                    waits=[(sEv, sEv.n)], incs=[(sIn, 16)])
                sch.op("vector", lambda e: e.tensor_scalar(
                    sstC[0:1, 16:16] if False else srow[0:1, :], srow[0:1, :],
                    sstC[0:1, 12:13], None, ALU.mult),
                    waits=[(sIn, sIn.n), (sEv, sEv.n)], incs=[(sEv, 1)])
                sch.op("scalar", lambda e: e.activation(srow[0:1, :], srow[0:1, :], AF.Exp),
                       waits=[(sEv, sEv.n)], incs=[(sSc, 1)])
                sch.op("tensor", lambda e: e.matmul(
                    psB0[:, :], ones1r[0:1, 0:128], srow[0:1, :],
                    start=True, stop=True), waits=[(sSc, sSc.n)], incs=[(sMM, 1)])
                for fc in range(NFC):
                    sch.op("vector", lambda e, fc=fc, dsti=dsti: e.tensor_copy(
                        dsti[:, fc * 128:(fc + 1) * 128], psB0[:, :]),
                        waits=([(sMM, sMM.n)] if fc == 0 else []),
                        incs=([(sEv, 1)] if fc == NFC - 1 else []))
            sch.op("vector", lambda e: e.tensor_scalar(
                cM2[:, :], cM2[:, :], scC[:, 0:1], None, ALU.mult),
                waits=[(sEv, sEv.n)], incs=[(sEv, 1)])
            sch.op("scalar", lambda e: e.activation(cM2[:, :], cM2[:, :], AF.Exp),
                   waits=[(sEv, sEv.n)], incs=[(sSc, 1)])
            sch.op("vector", lambda e: e.tensor_mul(cM2[:, :], cM2[:, :], cMsk[:, :]),
                   waits=[(sSc, sSc.n)], incs=[(sEv, 1)])
        tabEv = sEv.n
        tabSc = sSc.n
        tabMM = sMM.n

        with contextlib.ExitStack() as stSC:
            Wti = stSC.enter_context(nc.sbuf_tensor("Wti", [128, NFC * 1024], F32))
            wkch = stSC.enter_context(nc.sbuf_tensor("wkch", [128, 2 * 1024], BF16))
            vch = stSC.enter_context(nc.sbuf_tensor("vch", [128, 2 * 1024], BF16))
            vgf = stSC.enter_context(nc.sbuf_tensor("vgf", [128, 1024], BF16))
            rkbf = stSC.enter_context(nc.sbuf_tensor("rkbf", [128, 1024], BF16))
            rkf = stSC.enter_context(nc.sbuf_tensor("rkf", [128, 1024], F32))
            rkg = stSC.enter_context(nc.sbuf_tensor("rkg", [128, 1024], F32))
            wk_tm = stSC.enter_context(nc.sbuf_tensor("wk_tm", [128, 1024], BF16))
            vg_tm = stSC.enter_context(nc.sbuf_tensor("vg_tm", [128, 1024], BF16))
            smb = stSC.enter_context(nc.sbuf_tensor("smb", [128, 128], BF16))
            smtb = stSC.enter_context(nc.sbuf_tensor("smtb", [128, 128], BF16))
            rdtm = stSC.enter_context(nc.sbuf_tensor("rdtm", [128, 1024], F32))
            psT6 = stSC.enter_context(nc.psum_tensor("psT6", [128, 512], BF16))
            psR = stSC.enter_context(nc.psum_tensor("psR", [128, 1024], F32))
            psU = stSC.enter_context(nc.psum_tensor("psU", [128, 2 * 1024], F32))
            psT8 = stSC.enter_context(nc.psum_tensor("psT8", [128, 512], F32))
            sch.op("vector", lambda e: e.memset(Wti[:, :], 0.0),
                   waits=[(sEv, tabEv)], incs=[(sEv, 1)])
            wtiEv = sEv.n
            ld_rel = [tabMM, tabMM]
            vg_relE = [0]
            tm_rel = [0]
            psT6_rel = [0]
            psT8_rel = [0]
            psU_rel = [0, 0]
            psR_rel = [0]
            rk_rel = [0]
            rkg_relE = [0]
            stt_last = [0]
            for cc in range(NCH):
                cb = cc % 2
                ownc = cc >= OWN0
                oc = cc - OWN0
                sch.op("gpsimd", lambda e, cc=cc, cb=cb: e.dma_start(
                    out=wkch[:, cb * 1024:(cb + 1) * 1024].rearrange(
                        "p (f s) -> p f s", f=NFC),
                    in_=wks3[:, :, cc * 128:(cc + 1) * 128]),
                    waits=[(sMM, ld_rel[cb])], incs=[(sLd, 16)])
                sch.op("sync", lambda e, cc=cc, cb=cb: e.dma_start(
                    out=vch[:, cb * 1024:(cb + 1) * 1024].rearrange(
                        "p (f s) -> p f s", f=NFC),
                    in_=vsc3[:, :, cc * 128:(cc + 1) * 128]),
                    waits=[(sMM, ld_rel[cb]), (sEv, vg_relE[0])], incs=[(sLd, 16)])
                if ownc:
                    sch.op("gpsimd", lambda e, oc=oc: e.dma_start(
                        out=rkbf[:, :].rearrange("p (f s) -> p f s", f=NFC),
                        in_=rko3[:, :, oc * 128:(oc + 1) * 128]),
                        waits=[(sMM, rk_rel[0])], incs=[(sLd, 16)])
                    sch.op("sync", lambda e, oc=oc: e.dma_start(
                        out=rkf[:, :].rearrange("p (f s) -> p f s", f=NFC),
                        in_=rko3[:, :, oc * 128:(oc + 1) * 128]),
                        waits=[(sEv, rkg_relE[0])], incs=[(sLd, 16)])
                ld = sLd.n
                sch.op("vector", lambda e, cb=cb: e.tensor_mul(
                    vgf[:, :], vch[:, cb * 1024:(cb + 1) * 1024], gw8[:, :]),
                    waits=[(sLd, ld), (sMM, tm_rel[0])] +
                          ([(sEv, wtiEv)] if cc == 0 else []), incs=[(sEv, 1)])
                vg_relE[0] = sEv.n
                vgEv = sEv.n
                for (srcb, dstb, src_is_slot) in ((wkch, wk_tm, 1), (vgf, vg_tm, 0)):
                    for fc in range(NFC):
                        w0 = []
                        if fc % 4 == 0:
                            w0 = [(sEv, psT6_rel[0])]
                            if fc == 0:
                                w0 += [(sEv, vgEv), (sMM, tm_rel[0])]
                        off6 = cb * 1024 if src_is_slot else 0
                        sch.op("tensor", lambda e, srcb=srcb, fc=fc, off6=off6: e.transpose(
                            psT6[:, (fc % 4) * 128:(fc % 4) * 128 + 128],
                            srcb[:, off6 + fc * 128: off6 + fc * 128 + 128],
                            cIdb[:, :]),
                            waits=w0,
                            incs=([(sMM, 1)] if fc % 4 == 3 else []))
                        if fc % 4 == 3:
                            sch.op("scalar", lambda e, dstb=dstb, fc=fc: e.activation(
                                dstb[:, (fc - 3) * 128:(fc + 1) * 128], psT6[:, :],
                                AF.Identity),
                                waits=[(sMM, sMM.n)], incs=[(sSc, 1)])
                            sch.op("vector", None, waits=[(sSc, sSc.n)], incs=[(sEv, 1)])
                            psT6_rel[0] = sEv.n
                tmSc = sSc.n
                if ownc:
                    sch.op("vector", lambda e: e.tensor_mul(
                        rkg[:, :], rkf[:, :], gpt8[:, :]),
                        waits=[(sLd, ld), (sMM, rk_rel[0])] +
                              ([(sEv, wtiEv)] if oc == 0 else []), incs=[(sEv, 1)])
                    rkg_relE[0] = sEv.n
                    rkgEv = sEv.n
                    for ic in range(NFC):
                        for jh in range(2):
                            w0 = []
                            if ic == 0 and jh == 0:
                                w0 = [(sEv, rkgEv), (sEv, stt_last[0]),
                                      (sEv, psR_rel[0])]
                            sch.op("tensor", lambda e, ic=ic, jh=jh: e.matmul(
                                psR[:, jh * 512:(jh + 1) * 512],
                                rkg[:, ic * 128:(ic + 1) * 128],
                                Wti[:, ic * 1024 + jh * 512: ic * 1024 + jh * 512 + 512],
                                start=(ic == 0), stop=False),
                                waits=w0, incs=[(sMM, 1)])
                    for fc in range(NFC):
                        w0 = []
                        if fc == 0:
                            w0 = [(sLd, ld), (sEv, psT8_rel[0])]
                        sch.op("tensor", lambda e, fc=fc, cb=cb: e.matmul(
                            psT8[:, 0:128],
                            rkbf[:, fc * 128:(fc + 1) * 128],
                            wkch[:, cb * 1024 + fc * 128: cb * 1024 + fc * 128 + 128],
                            start=(fc == 0), stop=(fc == NFC - 1)),
                            waits=w0, incs=([(sMM, 1)] if fc == NFC - 1 else []))
                    rk_rel[0] = sMM.n
                    sch.op("vector", lambda e: e.tensor_mul(
                        smb[:, :], psT8[:, 0:128], cM2[:, :]),
                        waits=[(sMM, sMM.n)], incs=[(sEv, 1)])
                    smEv = sEv.n
                    sch.op("tensor", lambda e: e.transpose(
                        psT6[:, 0:128], smb[:, :], cIdb[:, :]),
                        waits=[(sEv, smEv), (sEv, psT6_rel[0])], incs=[(sMM, 1)])
                    sch.op("vector", lambda e: e.tensor_copy(smtb[:, :], psT6[:, 0:128]),
                           waits=[(sMM, sMM.n)], incs=[(sEv, 1)])
                    psT6_rel[0] = sEv.n
                    smtEv = sEv.n
                    for jh in range(2):
                        sch.op("tensor", lambda e, jh=jh: e.matmul(
                            psR[:, jh * 512:(jh + 1) * 512],
                            smtb[:, :], vg_tm[:, jh * 512:(jh + 1) * 512],
                            start=False, stop=True),
                            waits=([(sEv, smtEv), (sSc, tmSc)] if jh == 0 else []),
                            incs=[(sMM, 1)])
                    mmR = sMM.n
                    sch.op("vector", lambda e: e.tensor_copy(rdtm[:, :], psR[:, :]),
                           waits=[(sMM, mmR)], incs=[(sEv, 1)])
                    rdEv = sEv.n
                    psR_rel[0] = rdEv
                    for fc in range(NFC):
                        w0 = []
                        if fc % 4 == 0:
                            w0 = [(sEv, psT8_rel[0])]
                            if fc == 0:
                                w0.append((sEv, rdEv))
                        sch.op("tensor", lambda e, fc=fc: e.transpose(
                            psT8[:, (fc % 4) * 128:(fc % 4) * 128 + 128],
                            rdtm[:, fc * 128:(fc + 1) * 128], cIdf[:, :]),
                            waits=w0,
                            incs=([(sMM, 1)] if fc % 4 == 3 else []))
                        if fc % 4 == 3:
                            for f2 in range(fc - 3, fc + 1):
                                sch.op("scalar", lambda e, f2=f2, oc=oc: e.activation(
                                    readsfm[:, f2 * NQ + oc * 128: f2 * NQ + oc * 128 + 128],
                                    psT8[:, (f2 % 4) * 128:(f2 % 4) * 128 + 128],
                                    AF.Identity),
                                    waits=([(sMM, sMM.n)] if f2 % 4 == 0 else []),
                                    incs=([(sSc, 1)] if f2 % 4 == 3 else []))
                            sch.op("vector", None, waits=[(sSc, sSc.n)], incs=[(sEv, 1)])
                            psT8_rel[0] = sEv.n
                for ic in range(NFC):
                    us = ic % 2
                    for jh in range(2):
                        w0 = []
                        if jh == 0:
                            w0 = [(sEv, psU_rel[us]), (sSc, tmSc)]
                        sch.op("tensor", lambda e, ic=ic, jh=jh, us=us: e.matmul(
                            psU[:, us * 1024 + jh * 512: us * 1024 + jh * 512 + 512],
                            wk_tm[:, ic * 128:(ic + 1) * 128],
                            vg_tm[:, jh * 512:(jh + 1) * 512],
                            start=True, stop=True),
                            waits=w0, incs=([(sMM, 1)] if jh == 1 else []))
                    mmU = sMM.n
                    sch.op("vector", lambda e, ic=ic, us=us: e.scalar_tensor_tensor(
                        Wti[:, ic * 1024:(ic + 1) * 1024],
                        Wti[:, ic * 1024:(ic + 1) * 1024],
                        scC[:, 2:3],
                        psU[:, us * 1024:(us + 1) * 1024],
                        ALU.mult, ALU.add),
                        waits=[(sMM, mmU)], incs=[(sEv, 1)])
                    psU_rel[us] = sEv.n
                stt_last[0] = sEv.n
                tm_rel[0] = sMM.n
                ld_rel[cb] = sMM.n
            scanEv = sEv.n
            scanMM = sMM.n
            scanSc = sSc.n
            scanLd = sLd.n

        with contextlib.ExitStack() as stFN:
            wtb = stFN.enter_context(nc.sbuf_tensor("wtb", [128, NFC * 1024], F32))
            ltb = stFN.enter_context(nc.sbuf_tensor("ltb", [128, NFC * 1024], F32))
            ternbf = stFN.enter_context(nc.sbuf_tensor("ternbf", [128, NFC * 1024], BF16))
            wRb = stFN.enter_context(nc.sbuf_tensor("wRb", [128, NFC * 1024], BF16))
            vall3 = stFN.enter_context(nc.sbuf_tensor("vall3", [128, NFC * 512], F32))
            xp2 = stFN.enter_context(nc.sbuf_tensor("xp2", [128, NFC * 512], F32))
            x3bf = stFN.enter_context(nc.sbuf_tensor("x3bf", [128, NFC * 512], BF16))
            ttm = stFN.enter_context(nc.sbuf_tensor("ttm", [128, 1024], F32))
            tsq = stFN.enter_context(nc.sbuf_tensor("tsq", [128, 1024], F32))
            ot8 = stFN.enter_context(nc.sbuf_tensor("ot8", [128, 1024], F32))
            psA5 = stFN.enter_context(nc.psum_tensor("psA5", [128, 512], F32))
            psB5 = stFN.enter_context(nc.psum_tensor("psB5", [128, 512], F32))
            psT7 = stFN.enter_context(nc.psum_tensor("psT7", [128, 512], F32))
            bndC = [(sMM, scanMM), (sEv, scanEv), (sSc, scanSc), (sLd, scanLd)]
            sch.op("gpsimd", lambda e: e.dma_start(
                out=wRb[:, :].rearrange("p (k o) -> p k o", k=NFC),
                in_=WrTi.rearrange("(k p) o -> p k o", p=128)),
                waits=bndC, incs=[(sW, 16)])
            sch.op("sync", lambda e: e.dma_start(
                out=wtb[:, :].rearrange("p (k o) -> p k o", k=NFC),
                in_=wTti.rearrange("(k p) o -> p k o", p=128)),
                waits=bndC, incs=[(sW, 16)])
            wFN = sW.n
            sch.op("vector", lambda e: e.tensor_scalar(
                ltb[:, :], wtb[:, :], -1.0, None, ALU.mult),
                waits=[(sW, wFN)], incs=[(sEv, 1)])
            sch.op("vector", lambda e: e.tensor_tensor(
                ltb[:, :], ltb[:, :], wtb[:, :], ALU.max),
                waits=[(sEv, sEv.n)], incs=[(sEv, 1)])
            sch.op("vector", lambda e: e.reduce_sum(scB[:, 0:1], ltb[:, :], axis=AX.X),
                   waits=[(sEv, sEv.n)], incs=[(sEv, 1)])
            sch.op("vector", lambda e: e.memset(scB[:, 128:256], 0.0),
                   waits=[(sEv, sEv.n)], incs=[(sEv, 1)])
            sch.op("vector", lambda e: e.tensor_copy(scB[:, 128:129], scB[:, 0:1]),
                   waits=[(sEv, sEv.n)], incs=[(sEv, 1)])
            sch.op("tensor", lambda e: e.transpose(psT7[:, 0:128], scB[:, 128:256],
                                                   cIdf[:, :]),
                   waits=[(sEv, sEv.n)], incs=[(sMM, 1)])
            sch.op("vector", lambda e: e.reduce_sum(sstC[0:1, 20:21],
                                                    psT7[0:1, 0:128], axis=AX.X),
                   waits=[(sMM, sMM.n)], incs=[(sEv, 1)])
            sch.op("vector", lambda e: e.tensor_scalar(
                sstC[0:1, 21:22], sstC[0:1, 20:21], 0.7 / (1024.0 * 1024.0),
                None, ALU.mult),
                waits=[(sEv, sEv.n)], incs=[(sEv, 1)])
            sch.op("tensor", lambda e: e.matmul(
                psT7[:, 0:1], ones1r[0:1, 0:128], sstC[0:1, 21:22],
                start=True, stop=True), waits=[(sEv, sEv.n)], incs=[(sMM, 1)])
            sch.op("vector", lambda e: e.tensor_copy(scC[:, 4:5], psT7[:, 0:1]),
                   waits=[(sMM, sMM.n)], incs=[(sEv, 1)])
            sch.op("vector", lambda e: e.tensor_scalar(
                scC[:, 5:6], scC[:, 4:5], -1.0, None, ALU.mult),
                waits=[(sEv, sEv.n)], incs=[(sEv, 1)])
            sch.op("vector", lambda e: e.tensor_scalar(
                ltb[:, :], wtb[:, :], scC[:, 5:6], None, ALU.is_lt),
                waits=[(sEv, sEv.n)], incs=[(sEv, 1)])
            sch.op("vector", lambda e: e.scalar_tensor_tensor(
                ternbf[:, :], wtb[:, :], scC[:, 4:5], ltb[:, :],
                ALU.is_gt, ALU.subtract),
                waits=[(sEv, sEv.n)], incs=[(sEv, 1)])
            ternEv = sEv.n
            ps5_rel = [ternEv, ternEv]
            psT7_rel = [ternEv]
            ttm_rel = [0]
            vall3_relE = [0]
            x3_relS = [0]
            ot8_relO = [0]
            ot8_relS = [0]

            def plain_ln_tm(buf):
                sch.op("vector", lambda e: e.reduce_sum(
                    scB[:, 0:1], buf[:, :], axis=AX.X),
                    waits=[(sEv, sEv.n)], incs=[(sEv, 1)])
                sch.op("vector", lambda e: e.tensor_scalar(
                    scB[:, 8:9], scB[:, 0:1], -1.0 / D, None, ALU.mult),
                    waits=[(sEv, sEv.n)], incs=[(sEv, 1)])
                sch.op("vector", lambda e: e.tensor_add(
                    buf[:, :], buf[:, :], scB[:, 8:9].broadcast_to((128, D))),
                    waits=[(sEv, sEv.n)], incs=[(sEv, 1)])
                sch.op("vector", lambda e: e.tensor_mul(
                    tsq[:, :], buf[:, :], buf[:, :]),
                    waits=[(sEv, sEv.n)], incs=[(sEv, 1)])
                sch.op("vector", lambda e: e.reduce_sum(
                    scB[:, 1:2], tsq[:, :], axis=AX.X),
                    waits=[(sEv, sEv.n)], incs=[(sEv, 1)])
                sch.op("vector", lambda e: e.tensor_scalar(
                    scB[:, 4:5], scB[:, 1:2], 1.0 / D, EPS, ALU.mult, ALU.add),
                    waits=[(sEv, sEv.n)], incs=[(sEv, 1)])
                sch.op("scalar", lambda e: e.activation(
                    scB[:, 7:8], scB[:, 4:5], AF.Ln),
                    waits=[(sEv, sEv.n)], incs=[(sSc, 1)])
                sch.op("scalar", lambda e: e.activation(
                    scB[:, 3:4], scB[:, 7:8], AF.Exp, scale=-0.5),
                    waits=[(sSc, sSc.n)], incs=[(sSc, 1)])
                sch.op("vector", None, waits=[(sSc, sSc.n)], incs=[(sEv, 1)])
                sch.op("vector", lambda e: e.tensor_mul(
                    buf[:, :], buf[:, :], scB[:, 3:4].broadcast_to((128, D))),
                    waits=[(sEv, sEv.n)], incs=[(sEv, 1)])

            for pi in range(4):
                off = pi * 512
                sch.op("sync", lambda e, off=off: e.dma_start(
                    out=xp2[:, :].rearrange("p (f s) -> p f s", f=NFC),
                    in_=rko3[:, :, off:off + 512]),
                    waits=[(sEv, max(vall3_relE[0], ternEv)), (sOut, ot8_relO[0])],
                    incs=[(sLd, 16)])
                ldF = sLd.n
                for fco in range(NFC):
                    ps = psA5 if fco % 2 == 0 else psB5
                    for kc in range(NFC):
                        w0 = []
                        if kc == 0:
                            w0 = [(sEv, ps5_rel[fco % 2])]
                        sch.op("tensor", lambda e, ps=ps, kc=kc, fco=fco, off=off: e.matmul(
                            ps[:, :],
                            wRb[:, kc * 1024 + fco * 128: kc * 1024 + fco * 128 + 128],
                            readsfm[:, kc * NQ + off: kc * NQ + off + 512],
                            start=(kc == 0), stop=(kc == NFC - 1)),
                            waits=w0,
                            incs=([(sMM, 1)] if kc == NFC - 1 else []))
                    sch.op("vector", lambda e, ps=ps, fco=fco: e.scalar_tensor_tensor(
                        vall3[:, fco * 512:(fco + 1) * 512], ps[:, :], scC[:, 1:2],
                        xp2[:, fco * 512:(fco + 1) * 512], ALU.mult, ALU.add),
                        waits=[(sMM, sMM.n), (sLd, ldF), (sEv, vall3_relE[0]),
                               (sEv, sEv.n)],
                        incs=[(sEv, 1)])
                    sch.op("vector", lambda e, fco=fco: e.tensor_add(
                        vall3[:, fco * 512:(fco + 1) * 512],
                        vall3[:, fco * 512:(fco + 1) * 512],
                        xp2[:, fco * 512:(fco + 1) * 512]),
                        waits=[(sEv, sEv.n)], incs=[(sEv, 1)])
                    ps5_rel[fco % 2] = sEv.n
                roEv = sEv.n
                for ti in range(4):
                    loff = ti * 128
                    for fcx in range(NFC):
                        w0 = []
                        if fcx % 4 == 0:
                            w0 = [(sEv, psT7_rel[0])]
                            if fcx == 0:
                                w0 += [(sEv, roEv), (sMM, ttm_rel[0])]
                        sch.op("tensor", lambda e, fcx=fcx, loff=loff: e.transpose(
                            psT7[:, (fcx % 4) * 128:(fcx % 4) * 128 + 128],
                            vall3[:, fcx * 512 + loff: fcx * 512 + loff + 128],
                            cIdf[:, :]),
                            waits=w0,
                            incs=([(sMM, 1)] if fcx % 4 == 3 else []))
                        if fcx % 4 == 3:
                            sch.op("vector", lambda e, fcx=fcx: e.tensor_copy(
                                ttm[:, (fcx - 3) * 128:(fcx + 1) * 128], psT7[:, :]),
                                waits=[(sMM, sMM.n)], incs=[(sEv, 1)])
                            psT7_rel[0] = sEv.n
                    plain_ln_tm(ttm)
                    x3Ev = sEv.n
                    for fcx in range(NFC):
                        w0 = []
                        if fcx % 4 == 0:
                            w0 = [(sEv, psT7_rel[0]), (sSc, x3_relS[0])]
                            if fcx == 0:
                                w0.append((sEv, x3Ev))
                        sch.op("tensor", lambda e, fcx=fcx: e.transpose(
                            psT7[:, (fcx % 4) * 128:(fcx % 4) * 128 + 128],
                            ttm[:, fcx * 128:(fcx + 1) * 128], cIdf[:, :]),
                            waits=w0,
                            incs=([(sMM, 1)] if fcx % 4 == 3 else []))
                        if fcx % 4 == 3:
                            for f2 in range(fcx - 3, fcx + 1):
                                sch.op("vector", lambda e, f2=f2, loff=loff: e.tensor_copy(
                                    vall3[:, f2 * 512 + loff: f2 * 512 + loff + 128],
                                    psT7[:, (f2 % 4) * 128:(f2 % 4) * 128 + 128]),
                                    waits=([(sMM, sMM.n)] if f2 % 4 == 0 else []),
                                    incs=([(sEv, 1)] if f2 % 4 == 3 else []))
                            psT7_rel[0] = sEv.n
                    ttm_rel[0] = sMM.n
                    xfEv = sEv.n
                    for fcx in range(NFC):
                        sch.op("scalar", lambda e, fcx=fcx, loff=loff: e.activation(
                            x3bf[:, fcx * 512 + loff: fcx * 512 + loff + 128],
                            vall3[:, fcx * 512 + loff: fcx * 512 + loff + 128],
                            AF.Identity),
                            waits=([(sEv, xfEv)] if fcx == 0 else []),
                            incs=([(sSc, 1)] if fcx == NFC - 1 else []))
                    x3_relS[0] = sSc.n
                x3Sc = sSc.n
                for fco in range(NFC):
                    ps = psA5 if fco % 2 == 0 else psB5
                    for kc in range(NFC):
                        w0 = []
                        if kc == 0:
                            w0 = [(sEv, ps5_rel[fco % 2])]
                            if fco == 0:
                                w0.append((sSc, x3Sc))
                        sch.op("tensor", lambda e, ps=ps, kc=kc, fco=fco: e.matmul(
                            ps[:, :],
                            ternbf[:, kc * 1024 + fco * 128: kc * 1024 + fco * 128 + 128],
                            x3bf[:, kc * 512:(kc + 1) * 512],
                            start=(kc == 0), stop=(kc == NFC - 1)),
                            waits=w0,
                            incs=([(sMM, 1)] if kc == NFC - 1 else []))
                    sch.op("vector", lambda e, ps=ps, fco=fco: e.scalar_tensor_tensor(
                        xp2[:, fco * 512:(fco + 1) * 512], ps[:, :], scC[:, 3:4],
                        vall3[:, fco * 512:(fco + 1) * 512], ALU.mult, ALU.add),
                        waits=[(sMM, sMM.n), (sEv, sEv.n)], incs=[(sEv, 1)])
                    sch.op("vector", lambda e, fco=fco: e.tensor_scalar(
                        xp2[:, fco * 512:(fco + 1) * 512],
                        xp2[:, fco * 512:(fco + 1) * 512],
                        cBb[:, fco:fco + 1], None, ALU.add),
                        waits=[(sEv, sEv.n)], incs=[(sEv, 1)])
                    ps5_rel[fco % 2] = sEv.n
                teEv = sEv.n
                for ti in range(4):
                    loff = ti * 128
                    t = pi * 4 + ti
                    for fcx in range(NFC):
                        w0 = []
                        if fcx % 4 == 0:
                            w0 = [(sEv, psT7_rel[0])]
                            if fcx == 0:
                                w0 += [(sEv, teEv), (sMM, ttm_rel[0])]
                        sch.op("tensor", lambda e, fcx=fcx, loff=loff: e.transpose(
                            psT7[:, (fcx % 4) * 128:(fcx % 4) * 128 + 128],
                            xp2[:, fcx * 512 + loff: fcx * 512 + loff + 128],
                            cIdf[:, :]),
                            waits=w0,
                            incs=([(sMM, 1)] if fcx % 4 == 3 else []))
                        if fcx % 4 == 3:
                            sch.op("vector", lambda e, fcx=fcx: e.tensor_copy(
                                ttm[:, (fcx - 3) * 128:(fcx + 1) * 128], psT7[:, :]),
                                waits=[(sMM, sMM.n)], incs=[(sEv, 1)])
                            psT7_rel[0] = sEv.n
                    plain_ln_tm(ttm)
                    oEv = sEv.n
                    for fcx in range(NFC):
                        w0 = []
                        if fcx % 4 == 0:
                            w0 = [(sEv, psT7_rel[0]), (sOut, ot8_relO[0])]
                            if fcx == 0:
                                w0.append((sEv, oEv))
                        sch.op("tensor", lambda e, fcx=fcx: e.transpose(
                            psT7[:, (fcx % 4) * 128:(fcx % 4) * 128 + 128],
                            ttm[:, fcx * 128:(fcx + 1) * 128], cIdf[:, :]),
                            waits=w0,
                            incs=([(sMM, 1)] if fcx % 4 == 3 else []))
                        if fcx % 4 == 3:
                            for f2 in range(fcx - 3, fcx + 1):
                                sch.op("vector", lambda e, f2=f2: e.tensor_copy(
                                    ot8[:, f2 * 128:(f2 + 1) * 128],
                                    psT7[:, (f2 % 4) * 128:(f2 % 4) * 128 + 128]),
                                    waits=([(sMM, sMM.n)] if f2 % 4 == 0 else []),
                                    incs=([(sEv, 1)] if f2 % 4 == 3 else []))
                            psT7_rel[0] = sEv.n
                    ttm_rel[0] = sMM.n
                    otEv = sEv.n
                    sch.op("sync", lambda e, t=t: e.dma_start(
                        out=out3[:, :, t * 128:(t + 1) * 128],
                        in_=ot8[:, :].rearrange("p (f s) -> p f s", f=NFC)),
                        waits=[(sEv, otEv)], incs=[(sOut, 16)])
                    ot8_relO[0] = sOut.n
                vall3_relE[0] = teEv
            sch.op("sync", None, waits=[(sOut, sOut.n)])
            for En in ("vector", "tensor", "scalar", "gpsimd"):
                sch.op(En, None, waits=[(sOut, sOut.n)])

        with nc.Block("launchC") as blk:
            sch.emit(blk)
    _C_CACHE["nc"] = nc
    return nc


def run_C(inputs, resB):
    maps = host_prep_C(inputs, resB)
    nc = build_launch_C()
    res = run_bass_kernel_spmd(nc, maps, core_ids=list(range(8)))
    return res.results


def kernel(**inputs):
    resA = run_A(inputs)
    resB = run_B(inputs, resA)
    resC = run_C(inputs, resB)
    out = np.zeros((2, S, D), np.float32)
    for c in range(8):
        s2, r = divmod(c, 4)
        out[s2, r * NQ:(r + 1) * NQ] = _unfm(np.asarray(resC[c]["out_f"]), NQ)
    return out
